# version 05
# baseline (speedup 1.0000x reference)
import numpy as np
import ml_dtypes
import concourse.bass as bass
import concourse.mybir as mybir
from concourse.bass_utils import run_bass_kernel_spmd

F32 = mybir.dt.float32
BF16 = mybir.dt.bfloat16
AF = mybir.ActivationFunctionType
ALU = mybir.AluOpType
AX = mybir.AxisListType

D = 2048
HD = 128
H = 16
KH = 4
WIN = 512
TKV = 1536
TQ = 1024
NKT = TKV // 128
NQT = TQ // 128
II = 4096
NE = 8
LAM = float(1.0 / np.sqrt(128.0))
NEGINF = -1.0e30
EPS_RMS = 1e-6
EPS_LN = 1e-5

ENGINES = ("tensor", "vector", "scalar", "gpsimd", "sync")


class Emit:

    def __init__(self, sems):
        self.q = {e: [] for e in ENGINES}
        self.cnt = {s: 0 for s in sems}

    def op(self, eng, fn, waits=(), inc=None, amt=1):
        w = {}
        for s, v in waits:
            if v and v > 0:
                w[s] = max(w.get(s, 0), v)
        self.q[eng].append((fn, tuple(w.items()), inc, amt))
        if inc is not None:
            self.cnt[inc] += amt
            return self.cnt[inc]
        return None

    def val(self, sem):
        return self.cnt[sem]

    def replay(self, engobj, eng, semh):
        waited = {}
        for fn, waits, inc, amt in self.q[eng]:
            for s, v in waits:
                if waited.get(s, 0) < v:
                    engobj.wait_ge(semh[s], v)
                    waited[s] = v
            ins = fn(engobj)
            if inc is not None:
                ins.then_inc(semh[inc], amt)


class BankGuard:

    def __init__(self):
        self.last = {}

    @staticmethod
    def _banks(tname, off_b, size_b):
        return [(tname, b) for b in range(off_b // 2048, (off_b + size_b - 1) // 2048 + 1)]

    def write_waits(self, tname, off_el, n_el):
        out = []
        for k in self._banks(tname, off_el * 4, n_el * 4):
            if k in self.last:
                out.append(self.last[k])
        return out

    def set_reader(self, tname, off_el, n_el, sem, val):
        for k in self._banks(tname, off_el * 4, n_el * 4):
            self.last[k] = (sem, val)


def build_program(debug=False):
    nc = bass.Bass("TRN2", target_bir_lowering=False)

    xkv = nc.dram_tensor("xkv", [TKV, D], F32, kind="ExternalInput").ap()
    lnbc = nc.dram_tensor("lnbc", [128, D], F32, kind="ExternalInput").ap()
    wq = nc.dram_tensor("wq", [D, D], F32, kind="ExternalInput").ap()
    wk = nc.dram_tensor("wk", [D, KH * HD], F32, kind="ExternalInput").ap()
    wv = nc.dram_tensor("wv", [D, KH * HD], F32, kind="ExternalInput").ap()
    wo = nc.dram_tensor("wo", [D, D], F32, kind="ExternalInput").ap()
    rwT = nc.dram_tensor("rwT", [D, NE], F32, kind="ExternalInput").ap()
    gwT = nc.dram_tensor("gwT", [NE, D, II], F32, kind="ExternalInput").ap()
    uwT = nc.dram_tensor("uwT", [NE, D, II], F32, kind="ExternalInput").ap()
    dwT = nc.dram_tensor("dwT", [NE, II, D], F32, kind="ExternalInput").ap()
    cos4 = nc.dram_tensor("cos4", [TKV, 256], BF16, kind="ExternalInput").ap()
    sin4 = nc.dram_tensor("sin4", [TKV, 256], BF16, kind="ExternalInput").ap()
    masks = nc.dram_tensor("masks", [8, 128, 640], BF16, kind="ExternalInput").ap()
    eyef_in = nc.dram_tensor("eyef", [128, 128], F32, kind="ExternalInput").ap()
    eyebf_in = nc.dram_tensor("eyebf", [128, 128], BF16, kind="ExternalInput").ap()
    cons_in = nc.dram_tensor("cons", [128, 16], F32, kind="ExternalInput").ap()
    out = nc.dram_tensor("out", [TQ, D], F32, kind="ExternalOutput").ap()

    xkv_t = xkv.rearrange("(t p) c -> t p c", p=128)
    wq_v = wq.rearrange("(t p) c -> p t c", p=128)
    wk_v = wk.rearrange("(t p) c -> p t c", p=128)
    wv_v = wv.rearrange("(t p) c -> p t c", p=128)
    wo_v = wo.rearrange("(t p) c -> p t c", p=128)
    rw_v = rwT.rearrange("(t p) e -> p t e", p=128)
    gw_v = gwT.rearrange("e (t p) c -> e p t c", p=128)
    uw_v = uwT.rearrange("e (t p) c -> e p t c", p=128)
    dw_v = dwT.rearrange("e (t p) c -> e p t c", p=128)
    cos_v = cos4.rearrange("(t p) c -> p t c", p=128)
    sin_v = sin4.rearrange("(t p) c -> p t c", p=128)
    mask_v = masks.rearrange("j p t -> p j t")
    out_t = out.rearrange("(t p) c -> t p c", p=128)

    ARENA_SIZE = 225280 - 16384
    arena = nc.alloc_sbuf_tensor("manual_arena", [128, ARENA_SIZE],
                                 mybir.dt.uint8)
    A0 = nc.lookup_mloc(arena).addr
    assert A0 + ARENA_SIZE <= 228000, A0
    sb = {}
    off = [A0]

    def alloc(name, shape, dt, at=None):
        nbytes = int(np.prod(shape[1:])) * (2 if dt == BF16 else 4)
        o = off[0] if at is None else at
        if at is None:
            off[0] = o + ((nbytes + 63) // 64) * 64
        h = nc.alloc_sbuf_tensor_at(name, list(shape), dt, offset=o)
        sb[name] = h.ap()
        return sb[name]

    eye_f = alloc("eye_f", [128, 128], F32)
    eye_bf = alloc("eye_bf", [128, 128], BF16)
    cons = alloc("cons_sb", [128, 16], F32)
    rw_sb = alloc("rw_sb", [128, 128], F32)
    stats = alloc("stats", [128, 128], F32)
    w_sb = alloc("w_sb", [128, 64], F32)
    probs_sb = alloc("probs_sb", [128, 64], F32)
    lgT_sb = alloc("lgT_sb", [128, 64], F32)
    S_sb = alloc("S_sb", [128, 16], F32)
    scr8 = alloc("scr8", [128, 16], F32)
    keys_sb = alloc("keys_sb", [128, 64], F32)
    probs2 = alloc("probs2", [128, 64], F32)
    keys2 = alloc("keys2", [128, 64], F32)
    pm_scr = alloc("pm_scr", [128, 256], F32)
    pm2 = alloc("pm2", [128, 256], F32)
    S_all = alloc("S_all", [128, 32], F32)
    S_all2 = alloc("S_all2", [128, 32], F32)
    km_all = alloc("km_all", [128, 32], F32)
    km2 = alloc("km2", [128, 32], F32)
    tmp1 = alloc("tmp1", [128, 128], F32)
    tmp2 = alloc("tmp2", [128, 128], F32)
    silu_scr = alloc("silu_scr", [128, 1024], F32)
    lg_sb = alloc("lg_sb", [8, 1024], F32, at=off[0] - 4096)
    dbg_r_sb = alloc("dbg_r_sb", [128, 16], F32)
    g_ring = alloc("g_ring", [128, 2 * 2048], BF16)
    u_ring = alloc("u_ring", [128, 2 * 2048], BF16)
    E0 = ((off[0] + 511) // 512) * 512

    xnT = alloc("xnT", [128, 16 * TKV], BF16, at=E0)
    aoT = alloc("aoT", [128, 16 * 1024], BF16, at=E0)
    x2T_bf = alloc("x2T_bf", [128, 16 * 1024], BF16, at=E0)
    probsT = alloc("probsT", [128, 2 * 8 * 512], BF16, at=E0 + 32768)
    dn_ring = alloc("dn_ring", [128, 4 * 2048], BF16, at=E0 + 32768)
    x_ring = alloc("x_ring", [128, 2 * 2048], F32, at=E0 + 49152)
    xn_ring = alloc("xn_ring", [128, 2 * 2048], BF16, at=E0 + 65536)
    ln_bc = alloc("ln_bc", [128, 2048], F32, at=E0 + 73728)
    wq_ring = alloc("wq_ring", [128, 3 * 4096], BF16, at=E0 + 49152)
    wo_ring = alloc("wo_ring", [128, 3 * 4096], BF16, at=E0 + 49152)
    cos_sb = alloc("cos_sb", [128, 12 * 256], BF16, at=E0 + 73728)
    sin_sb = alloc("sin_sb", [128, 12 * 256], BF16, at=E0 + 79872)
    sq_scr2 = alloc("sq_scr2", [128, 2048], BF16, at=E0 + 81920)
    sq_scr = alloc("sq_scr", [128, 2048], BF16, at=E0 + 87040)
    xcol_ring = alloc("xcol_ring", [128, 4 * 256], F32, at=E0 + 73728)
    qk_ring = alloc("qk_ring", [128, 2 * 256], BF16, at=E0 + 86016)
    mask_sb = alloc("mask_sb", [128, 8 * 640], BF16, at=E0 + 87040)
    ms_sb = alloc("ms_sb", [128, 2 * 640], F32, at=E0 + 100352)
    ex_sb = alloc("ex_sb", [128, 2 * 640], F32, at=E0 + 105472)
    pb_sb = alloc("pb_sb", [128, 2 * 640], BF16, at=E0 + 110592)
    x2Tf = alloc("x2Tf", [128, 16 * 1024], F32, at=E0 + 49152)
    hidT = alloc("hidT", [128, 32 * 1024], BF16, at=E0 + 49152)
    qT = alloc("qT", [128, 16 * 1024], BF16, at=E0 + 114688)
    kT = alloc("kT", [128, 4 * TKV], BF16, at=E0 + 147456)
    v_sb = alloc("v_sb", [128, 12 * 512], BF16, at=E0 + 159744)
    acc = alloc("acc", [128, 8 * 2048], F32, at=E0 + 114688)
    assert E0 + 114688 + 8 * 2048 * 4 <= 229300, E0

    sems = (["pe", "v", "s", "st", "dbg", "cst", "msk"]
            + ["xr%d" % i for i in range(2)] + ["xc%d" % i for i in range(4)]
            + ["wr%d" % i for i in range(3)] + ["gr%d" % i for i in range(2)]
            + ["ur%d" % i for i in range(2)] + ["dr%d" % i for i in range(4)])
    em = Emit(sems)
    taps = {}

    def tap(name, ap, shape, dt, waits):
        if not debug:
            return 0
        t = nc.dram_tensor("dbg_" + name, list(shape), dt, kind="ExternalOutput").ap()
        return em.op("sync", lambda sy, d=t, s_=ap: sy.dma_start(d, s_),
                     waits=waits, inc="dbg", amt=16)
    bg = BankGuard()
    slot_order = [0, 2, 4, 6, 1, 3, 5, 7]

    import contextlib
    _es = contextlib.ExitStack()
    with _es:
        PAh = _es.enter_context(nc.psum_tensor("PA", [128, 2048], F32))
        PBh = _es.enter_context(nc.psum_tensor("PB", [128, 2048], F32))
        semh = {name: _es.enter_context(nc.semaphore("sem_" + name))
                for name in sems}
        PA = PAh[:, :]
        PB = PBh[:, :]
        PT = [PB[:, 1024:1536].bitcast(BF16), PB[:, 1536:2048].bitcast(BF16)]

        xd_const = 0
        for dst, src in (
            (eye_f, eyef_in), (eye_bf, eyebf_in), (cons, cons_in), (ln_bc, lnbc),
            (rw_sb.rearrange("p (t e) -> p t e", e=8), rw_v),
        ):
            xd_const = em.op("sync", lambda sy, d=dst, s_=src: sy.dma_start(d, s_),
                             inc="cst", amt=16)

        v_stt = [0] * NKT
        s_rms = [0] * NKT
        s_sq = [0] * NKT
        v_red = [0] * NKT
        dbg_t0 = [0]
        pe_xnt = [0] * NKT
        xd_x = [0] * NKT
        for tk in range(NKT):
            sl = tk % 2
            ws = []
            if tk >= 2:
                ws = [("v", v_stt[tk - 2]), ("s", s_rms[tk - 2])]
            xd_x[tk] = em.op("sync", lambda sy, t=tk, s0=sl: sy.dma_start(
                x_ring[:, s0 * 2048:(s0 + 1) * 2048], xkv_t[t]), waits=ws,
                inc="xr%d" % sl, amt=16)
            pcol = (tk % 2) * 8
            ws = [("xr%d" % sl, xd_x[tk])]
            if tk >= 2:
                ws.append(("v", v_red[tk - 2]))
                if debug and tk == 2:
                    ws.append(("dbg", dbg_t0[0]))
            s_sq[tk] = em.op("scalar", lambda s_, s0=sl, pc=pcol,
                             q=(sq_scr if sl == 0 else sq_scr2): s_.activation(
                q[:, 0:2048], x_ring[:, s0 * 2048:(s0 + 1) * 2048], AF.Square),
                waits=ws, inc="s")
            ws = [("s", s_sq[tk])]
            if tk >= 2:
                ws.append(("s", s_rms[tk - 2]))
            v_red[tk] = em.op("vector", lambda v, pc=pcol,
                              q=(sq_scr if sl == 0 else sq_scr2): v.tensor_reduce(
                stats[:, pc:pc + 1], q[:, 0:2048], AX.X, ALU.add), waits=ws, inc="v")
            em.op("scalar", lambda s_, pc=pcol: s_.activation(
                stats[:, pc + 1:pc + 2], stats[:, pc:pc + 1], AF.Ln, scale=1.0 / D,
                bias=cons[:, 0:1]), waits=[("v", v_red[tk])])
            s_rms[tk] = em.op("scalar", lambda s_, pc=pcol: s_.activation(
                stats[:, pc + 2:pc + 3], stats[:, pc + 1:pc + 2], AF.Exp, scale=-0.5),
                inc="s")
            if tk == 0:
                dbg_t0[0] = tap("sq0", sq_scr[:, :], [128, 2048], BF16,
                                [("s", s_sq[0]), ("v", v_red[0])])
                dbg_t0[0] = max(dbg_t0[0], tap("st0", stats[:, 0:16], [128, 16], F32,
                                               [("s", s_rms[0])]))
            if debug:
                em.op("scalar", lambda s_, pc=pcol, t=tk: s_.copy(
                    dbg_r_sb[:, t:t + 1], stats[:, pc + 2:pc + 3]))
            ws = [("s", s_rms[tk]), ("cst", xd_const)]
            if tk >= 2:
                ws.append(("pe", pe_xnt[tk - 2]))
            v_stt[tk] = em.op("vector", lambda v, s0=sl, pc=pcol: v.scalar_tensor_tensor(
                xn_ring[:, s0 * 2048:(s0 + 1) * 2048],
                x_ring[:, s0 * 2048:(s0 + 1) * 2048],
                stats[:, pc + 2:pc + 3], ln_bc[:, :], op0=ALU.mult, op1=ALU.mult),
                waits=ws, inc="v")
            for half in range(2):
                pt = PT[half]
                pto = 1024 + half * 512
                ws = bg.write_waits("PB", pto, 512) + [("v", v_stt[tk])]
                cc = None
                for dc8 in range(8):
                    dc = half * 8 + dc8
                    cc = em.op("tensor", lambda t, s0=sl, d=dc, d8=dc8, p=pt: t.transpose(
                        p[:, d8 * 128:(d8 + 1) * 128],
                        xn_ring[:, s0 * 2048 + d * 128: s0 * 2048 + (d + 1) * 128],
                        eye_bf[:, :]),
                        waits=ws if dc8 == 0 else (), inc="pe" if dc8 == 7 else None)
                pe_xnt[tk] = cc
                vv = em.op("vector", lambda v, t=tk, h=half, p=pt: v.tensor_copy(
                    xnT.rearrange("p (dc q) -> p dc q", q=TKV)[:, h * 8:(h + 1) * 8,
                                                              t * 128:(t + 1) * 128],
                    p.rearrange("p (dc q) -> p dc q", q=128)[:, :, :]),
                    waits=[("pe", pe_xnt[tk])], inc="v")
                bg.set_reader("PB", pto, 512, "v", vv)
        v_a1_done = em.val("v")
        s_a1_done = em.val("s")
        pe_a1_done = em.val("pe")
        dbg_xnT = tap("xnT", xnT[:, :], [128, 16 * TKV], BF16, [("v", v_a1_done)])
        tap("r", dbg_r_sb[:, :], [128, 16], F32, [("s", s_a1_done), ("v", v_a1_done)])
        tap("lnbc1", ln_bc[:, :], [128, 2048], F32, [("v", v_a1_done)])

        em.op("sync", lambda sy: sy.dma_start(
            mask_sb.rearrange("p (j t) -> p j t", t=640), mask_v),
            waits=[("v", v_a1_done), ("s", s_a1_done)], inc="msk", amt=16)
        em.op("sync", lambda sy: sy.dma_start(
            cos_sb.rearrange("p (t c) -> p t c", c=256), cos_v),
            waits=[("v", v_a1_done)], inc="msk", amt=16)
        xd_cs = em.op("sync", lambda sy: sy.dma_start(
            sin_sb.rearrange("p (t c) -> p t c", c=256), sin_v), inc="msk", amt=16)

        wq_dma = {}
        wq_last_mm = {}
        qk_last_tr = {}
        proj_i = [0]

        def emit_weight_halfgroup(key, hseq, src_view):
            slot = hseq % 3
            ws = [("pe", wq_last_mm.get(slot, 0))]
            if hseq < 3:
                ws += [("v", v_a1_done), ("s", s_a1_done), ("pe", pe_a1_done)]
            wq_dma[key] = em.op("gpsimd", lambda g, sv=src_view, s0=slot: g.dma_start(
                wq_ring.rearrange("p (s t c) -> p s t c", s=3, c=256)[:, s0, :, :], sv),
                waits=ws, inc="wr%d" % slot, amt=16)
            return slot

        def emit_proj_unit(key, slot, tlist, dest_fn, rope):
            for tt in tlist:
                s8 = slot_order[proj_i[0] % 8]
                po = s8 * 256
                ws = bg.write_waits("PA", po, 256) + [
                    ("wr%d" % slot, wq_dma[key]), ("v", v_a1_done)]
                pe_stop = None
                for dc in range(16):
                    pe_stop = em.op("tensor", lambda t, d=dc, s0=slot, t0=tt, p0=po: t.matmul(
                        PA[:, p0:p0 + 256],
                        xnT[:, d * TKV + t0 * 128: d * TKV + (t0 + 1) * 128],
                        wq_ring[:, s0 * 4096 + d * 256: s0 * 4096 + (d + 1) * 256],
                        start=(d == 0), stop=(d == 15)),
                        waits=ws if dc == 0 else (), inc="pe" if dc == 15 else None)
                wq_last_mm[slot] = pe_stop
                if rope:
                    qsl = proj_i[0] % 2
                    pe_e = PA[:, po:po + 256].rearrange("p (g c) -> p g c", c=128)[:, :, 0:64]
                    pe_o = PA[:, po:po + 256].rearrange("p (g c) -> p g c", c=128)[:, :, 64:128]
                    cs = cos_sb[:, tt * 256: tt * 256 + 128].rearrange("p (g c) -> p g c", c=64)
                    sn = sin_sb[:, tt * 256: tt * 256 + 128].rearrange("p (g c) -> p g c", c=64)
                    dst = qk_ring[:, qsl * 256:(qsl + 1) * 256].rearrange("p (g c) -> p g c", c=128)
                    t1v = tmp1.rearrange("p (g c) -> p g c", c=64)[:, 0:2, :]
                    t2v = tmp2.rearrange("p (g c) -> p g c", c=64)[:, 0:2, :]
                    ws2 = [("pe", pe_stop), ("msk", xd_cs),
                           ("pe", qk_last_tr.get(qsl, 0))]
                    em.op("vector", lambda v, a=pe_e, b=cs, t=t1v: v.tensor_tensor(t, a, b, op=ALU.mult), waits=ws2)
                    em.op("vector", lambda v, a=pe_o, b=sn, t=t2v: v.tensor_tensor(t, a, b, op=ALU.mult))
                    em.op("vector", lambda v, d=dst, t=t1v, u=t2v: v.tensor_tensor(
                        d[:, :, 0:64], t, u, op=ALU.subtract))
                    em.op("vector", lambda v, a=pe_e, b=sn, t=t1v: v.tensor_tensor(t, a, b, op=ALU.mult))
                    em.op("vector", lambda v, a=pe_o, b=cs, t=t2v: v.tensor_tensor(t, a, b, op=ALU.mult))
                    vv = em.op("vector", lambda v, d=dst, t=t1v, u=t2v: v.tensor_tensor(
                        d[:, :, 64:128], t, u, op=ALU.add), inc="v")
                    bg.set_reader("PA", po, 256, "v", vv)
                    half = proj_i[0] % 2
                    pt = PT[half]
                    pto = 1024 + half * 512
                    ws3 = bg.write_waits("PB", pto, 512) + [("v", vv)]
                    cc = None
                    for g2 in range(2):
                        cc = em.op("tensor", lambda t, q=qsl, g=g2, p=pt: t.transpose(
                            p[:, g * 128:(g + 1) * 128],
                            qk_ring[:, q * 256 + g * 128: q * 256 + (g + 1) * 128],
                            eye_bf[:, :]),
                            waits=ws3 if g2 == 0 else (), inc="pe" if g2 == 1 else None)
                    qk_last_tr[qsl] = cc
                    vv2 = em.op("vector", lambda v, p=pt, t0=tt, df=dest_fn: v.tensor_copy(
                        df(t0), p.rearrange("p (g q) -> p g q", q=128)[:, 0:2, :]),
                        waits=[("pe", cc)], inc="v")
                    bg.set_reader("PB", pto, 512, "v", vv2)
                else:
                    vv = em.op("vector", lambda v, t0=tt, p0=po, df=dest_fn: v.tensor_copy(
                        df(t0), PA[:, p0:p0 + 256]), waits=[("pe", pe_stop)], inc="v")
                    bg.set_reader("PA", po, 256, "v", vv)
                proj_i[0] += 1

        qT_v = qT.rearrange("p (h q) -> p h q", q=1024)
        kT_v = kT.rearrange("p (h q) -> p h q", q=TKV)
        for hp in range(8):
            slot = emit_weight_halfgroup(("q", hp), hp, wq_v[:, :, hp * 256:(hp + 1) * 256])
            emit_proj_unit(("q", hp), slot, range(4, 12),
                           (lambda t0, h2=hp * 2: qT_v[:, h2:h2 + 2, (t0 - 4) * 128:(t0 - 3) * 128]),
                           rope=True)
        for kp in range(2):
            slot = emit_weight_halfgroup(("k", kp), 8 + kp, wk_v[:, :, kp * 256:(kp + 1) * 256])
            emit_proj_unit(("k", kp), slot, range(NKT),
                           (lambda t0, h2=kp * 2: kT_v[:, h2:h2 + 2, t0 * 128:(t0 + 1) * 128]),
                           rope=True)
        for kp in range(2):
            slot = emit_weight_halfgroup(("v", kp), 10 + kp, wv_v[:, :, kp * 256:(kp + 1) * 256])
            emit_proj_unit(("v", kp), slot, range(NKT),
                           (lambda t0, k0=kp: v_sb[:, t0 * 512 + k0 * 256: t0 * 512 + k0 * 256 + 256]),
                           rope=False)

        pe_a2_done = em.val("pe")
        v_a2_done = em.val("v")
        dbg_qT = tap("qT", qT[:, :], [128, 16 * 1024], BF16, [("v", v_a2_done)])
        dbg_kT = tap("kT", kT[:, :], [128, 4 * TKV], BF16, [("v", v_a2_done)])
        dbg_v = tap("v_sb", v_sb[:, :], [128, 12 * 512], BF16, [("v", v_a2_done)])

        s_exp_prev = {}
        v_pb_prev = {}
        dbg_h0 = [0]
        pb_last_pe = {}
        probsT_attn_last = {}
        sc_i = [0]
        pT_v = probsT.rearrange("p (b c q) -> p b c q", b=2, q=512)
        for j in range(2):
            for h in range(H):
                kh = h // 4
                buf = (j * H + h) % 2
                for qc in range(4):
                    si = sc_i[0] % 2
                    sc_i[0] += 1
                    so = si * 1024
                    pcol = 32 + si * 8
                    ws = bg.write_waits("PA", so, 640) + [("v", v_a2_done)]
                    kbase = kh * TKV + j * 512 + qc * 128
                    qbase = h * 1024 + j * 512 + qc * 128
                    em.op("tensor", lambda t, qb=qbase, kb=kbase, o=so: t.matmul(
                        PA[:, o:o + 512], qT[:, qb:qb + 128], kT[:, kb:kb + 512],
                        start=True, stop=True), waits=ws)
                    pe_sc = em.op("tensor", lambda t, qb=qbase, kb=kbase, o=so: t.matmul(
                        PA[:, o + 512:o + 640], qT[:, qb:qb + 128],
                        kT[:, kb + 512:kb + 640], start=True, stop=True), inc="pe")
                    msl = ms_sb[:, si * 640:(si + 1) * 640]
                    exl = ex_sb[:, si * 640:(si + 1) * 640]
                    pbl = pb_sb[:, si * 640:(si + 1) * 640]
                    mko = (j * 4 + qc) * 640
                    ws = [("pe", pe_sc), ("msk", xd_cs),
                          ("s", s_exp_prev.get(si, 0)), ("pe", pb_last_pe.get(si, 0))]
                    ws.append(("dbg", dbg_h0[0]))
                    em.op("vector", lambda v, m=msl, o=so, mo=mko: v.scalar_tensor_tensor(
                        m, PA[:, o:o + 640], LAM, mask_sb[:, mo:mo + 640],
                        op0=ALU.mult, op1=ALU.add), waits=ws)
                    vv = em.op("vector", lambda v, m=msl, pc=pcol: v.tensor_reduce(
                        stats[:, pc:pc + 1], m, AX.X, ALU.max, negate=True), inc="v")
                    bg.set_reader("PA", so, 640, "v", vv)
                    sc = em.op("scalar", lambda s_, m=msl, e=exl, pc=pcol: s_.activation(
                        e, m, AF.Exp, bias=stats[:, pc:pc + 1]),
                        waits=[("v", vv), ("v", v_pb_prev.get(si, 0))], inc="s")
                    s_exp_prev[si] = sc
                    vv = em.op("vector", lambda v, e=exl, pc=pcol: v.tensor_reduce(
                        stats[:, pc + 2:pc + 3], e, AX.X, ALU.add),
                        waits=[("s", sc)], inc="v")
                    sc2 = em.op("scalar", lambda s_, pc=pcol: s_.activation(
                        stats[:, pc + 4:pc + 5], stats[:, pc + 2:pc + 3], AF.Copy),
                        waits=[("v", vv)], inc="s")
                    vv = em.op("vector", lambda v, pc=pcol: v.reciprocal(
                        stats[:, pc + 3:pc + 4], stats[:, pc + 4:pc + 5]),
                        waits=[("s", sc2)], inc="v")
                    sc2 = em.op("scalar", lambda s_, pc=pcol: s_.activation(
                        stats[:, pc + 5:pc + 6], stats[:, pc + 3:pc + 4], AF.Copy),
                        waits=[("v", vv)], inc="s")
                    vv = em.op("vector", lambda v, e=exl, p=pbl, pc=pcol: v.tensor_scalar_mul(
                        p, e, stats[:, pc + 5:pc + 6]), waits=[("s", sc2)], inc="v")
                    v_pb_prev[si] = vv
                    pt = PT[si]
                    pto = 1024 + si * 512
                    ws = bg.write_waits("PB", pto, 512) + [("v", vv)]
                    cc = None
                    for t5 in range(5):
                        cc = em.op("tensor", lambda t, p=pt, q=t5, pb=pbl: t.transpose(
                            p[:, q * 128:(q + 1) * 128], pb[:, q * 128:(q + 1) * 128],
                            eye_bf[:, :]), waits=ws if t5 == 0 else (),
                            inc="pe" if t5 == 4 else None)
                    pb_last_pe[si] = cc
                    vv2 = em.op("vector", lambda v, b=buf, q=qc, p=pt: v.tensor_copy(
                        pT_v[:, b, q:q + 5, q * 128:(q + 1) * 128],
                        p.rearrange("p (c q) -> p c q", q=128)[:, 0:5, :]),
                        waits=[("pe", cc), ("pe", probsT_attn_last.get(buf, 0)),
                               ("dbg", dbg_xnT)],
                        inc="v")
                    bg.set_reader("PB", pto, 512, "v", vv2)
                v_probsT = em.val("v")
                avo = buf * 512
                ws = bg.write_waits("PB", avo, 512) + [("v", v_probsT)]
                cc = None
                for c8 in range(8):
                    lo = max(0, c8 - 4) * 128
                    hi = (min(3, c8) + 1) * 128
                    vt = j * 4 + c8
                    cc = em.op("tensor", lambda t, o=avo, c0=c8, l=lo, h2=hi, v0=vt,
                               k0=kh, b=buf: t.matmul(
                        PB[:, o + l:o + h2],
                        v_sb[:, v0 * 512 + k0 * 128: v0 * 512 + (k0 + 1) * 128],
                        pT_v[:, b, c0, l:h2], start=(c0 == 0), stop=(c0 == 7)),
                        waits=ws if c8 == 0 else (), inc="pe" if c8 == 7 else None)
                probsT_attn_last[buf] = cc
                vv = em.op("vector", lambda v, o=avo, h0=h, j0=j: v.tensor_copy(
                    aoT[:, h0 * 1024 + j0 * 512: h0 * 1024 + j0 * 512 + 512],
                    PB[:, o:o + 512]), waits=[("pe", cc), ("dbg", dbg_xnT)], inc="v")
                bg.set_reader("PB", avo, 512, "v", vv)
                if j == 0 and h == 0:
                    dbg_h0[0] = tap("ms00", ms_sb[:, :], [128, 2 * 640], F32, [("v", vv)])
                    dbg_h0[0] = tap("maskA3", mask_sb[:, :], [128, 8 * 640], BF16, [("v", vv)]) or dbg_h0[0]
                    dbg_h0[0] = tap("statsA3", stats[:, 32:48], [128, 16], F32, [("v", vv)]) or dbg_h0[0]
                    dbg_h0[0] = tap("ex00", ex_sb[:, :], [128, 2 * 640], F32, [("v", vv)]) or dbg_h0[0]
                    dbg_h0[0] = tap("pb00", pb_sb[:, :], [128, 2 * 640], BF16, [("v", vv)]) or dbg_h0[0]
                    dbg_h0[0] = tap("pT00", probsT[:, 0:4096], [128, 4096], BF16, [("v", vv)]) or dbg_h0[0]
                    dbg_h0[0] = tap("ao00", aoT[:, 0:512], [128, 512], BF16, [("v", vv)]) or dbg_h0[0]
        pe_a3_done = em.val("pe")
        v_a3_done = em.val("v")
        s_a3_done = em.val("s")
        dbg_aoT = tap("aoT", aoT[:, :], [128, 16 * 1024], BF16, [("v", v_a3_done)])
        dbg_qkv = max(dbg_qT, dbg_kT, dbg_v) if debug else 0

        wo_dma = {}
        wo_last_mm = {}
        xc_dma = {}
        v_accw = {}
        wo_i = [0]
        v_ln1 = [0] * NQT
        for hg in range(8):
            slot = hg % 3
            ws = [("pe", wo_last_mm.get(slot, 0))]
            if hg < 3:
                ws += [("pe", pe_a2_done), ("v", v_a2_done)]
            else:
                ws += [("v", v_a3_done), ("s", s_a3_done), ("pe", pe_a3_done)]
            wo_dma[hg] = em.op("gpsimd", lambda g, h0=hg, s0=slot: g.dma_start(
                wo_ring.rearrange("p (s t c) -> p s t c", s=3, c=256)[:, s0, :, :],
                wo_v[:, :, h0 * 256:(h0 + 1) * 256]), waits=ws,
                inc="wr%d" % slot, amt=16)
            for tq in range(NQT):
                seq = hg * 8 + tq
                xsl = tq % 4
                ws = [("v", v_a2_done)] if seq < 4 else [("v", v_accw.get(seq - 4, 0))]
                if seq < 4:
                    ws.append(("pe", pe_a3_done))
                xc_dma[seq] = em.op("sync", lambda sy, t=tq, h0=hg, s0=xsl: sy.dma_start(
                    xcol_ring[:, s0 * 256:(s0 + 1) * 256],
                    xkv_t[4 + t][:, h0 * 256:(h0 + 1) * 256]), waits=ws,
                    inc="xc%d" % xsl, amt=16)
                s8 = slot_order[wo_i[0] % 8]
                wo_i[0] += 1
                po = s8 * 256
                ws = bg.write_waits("PA", po, 256) + [("wr%d" % slot, wo_dma[hg]),
                                                      ("v", v_a3_done)]
                cc = None
                for hc in range(16):
                    cc = em.op("tensor", lambda t, h2=hc, t0=tq, s0=slot, p0=po: t.matmul(
                        PA[:, p0:p0 + 256],
                        aoT[:, h2 * 1024 + t0 * 128: h2 * 1024 + (t0 + 1) * 128],
                        wo_ring[:, s0 * 4096 + h2 * 256: s0 * 4096 + (h2 + 1) * 256],
                        start=(h2 == 0), stop=(h2 == 15)),
                        waits=ws if hc == 0 else (), inc="pe" if hc == 15 else None)
                wo_last_mm[slot] = cc
                vv = em.op("vector", lambda v, t0=tq, h0=hg, s0=xsl, p0=po: v.tensor_tensor(
                    acc[:, t0 * 2048 + h0 * 256: t0 * 2048 + (h0 + 1) * 256],
                    PA[:, p0:p0 + 256], xcol_ring[:, s0 * 256:(s0 + 1) * 256],
                    op=ALU.add), waits=[("pe", cc), ("xc%d" % xsl, xc_dma[seq]),
                                        ("dbg", dbg_qkv)], inc="v")
                bg.set_reader("PA", po, 256, "v", vv)
                v_accw[seq] = vv
        for tq in range(NQT):
            at = acc[:, tq * 2048:(tq + 1) * 2048]
            pcol = 48 + (tq % 2) * 8
            vv = em.op("vector", lambda v, a=at, pc=pcol: v.tensor_reduce(
                stats[:, pc:pc + 1], a, AX.X, ALU.add, negate=True),
                waits=[("v", v_accw[7 * 8 + tq])], inc="v")
            sm = em.op("scalar", lambda s_, pc=pcol: s_.activation(
                stats[:, pc + 1:pc + 2], stats[:, pc:pc + 1], AF.Copy,
                scale=1.0 / D), waits=[("v", vv)], inc="s")
            vv = em.op("vector", lambda v, a=at, pc=pcol: v.tensor_scalar_add(
                a, a, stats[:, pc + 1:pc + 2]), waits=[("s", sm)], inc="v")
            s1 = em.op("scalar", lambda s_, a=at,
                       q=(sq_scr if tq % 2 == 0 else sq_scr2): s_.activation(
                q[:, 0:2048], a, AF.Square), waits=[("v", vv)], inc="s")
            vr = em.op("vector", lambda v, pc=pcol,
                       q=(sq_scr if tq % 2 == 0 else sq_scr2): v.tensor_reduce(
                stats[:, pc + 2:pc + 3], q[:, 0:2048], AX.X, ALU.add),
                waits=[("s", s1)], inc="v")
            em.op("scalar", lambda s_, pc=pcol: s_.activation(
                stats[:, pc + 3:pc + 4], stats[:, pc + 2:pc + 3], AF.Ln,
                scale=1.0 / D, bias=cons[:, 1:2]), waits=[("v", vr)])
            sc = em.op("scalar", lambda s_, pc=pcol: s_.activation(
                stats[:, pc + 4:pc + 5], stats[:, pc + 3:pc + 4], AF.Exp, scale=-0.5),
                inc="s")
            v_ln1[tq] = em.op("vector", lambda v, a=at, pc=pcol: v.tensor_scalar_mul(
                a, a, stats[:, pc + 4:pc + 5]), waits=[("s", sc)], inc="v")
        pe_wo_done = em.val("pe")
        s_ln1_done = em.val("s")
        dbg_x2 = tap("x2", acc[:, :], [128, 8 * 2048], F32, [("v", em.val("v"))])

        x2Tf_v = x2Tf.rearrange("p (dc q) -> p dc q", q=1024)
        for tq in range(NQT):
            for r4 in range(4):
                reg = r4 % 2
                ro = reg * 512
                ws = bg.write_waits("PB", ro, 512) + [("v", v_ln1[tq]),
                                                      ("pe", pe_wo_done)]
                cc = None
                for d4 in range(4):
                    dc = r4 * 4 + d4
                    cc = em.op("tensor", lambda t, t0=tq, d=dc, d0=d4, o=ro: t.transpose(
                        PB[:, o + d0 * 128: o + (d0 + 1) * 128],
                        acc[:, t0 * 2048 + d * 128: t0 * 2048 + (d + 1) * 128],
                        eye_f[:, :]), waits=ws if d4 == 0 else (),
                        inc="pe" if d4 == 3 else None)
                vv = em.op("vector", lambda v, t0=tq, r=r4, o=ro: v.tensor_copy(
                    x2Tf_v[:, r * 4:(r + 1) * 4, t0 * 128:(t0 + 1) * 128],
                    PB[:, o:o + 512].rearrange("p (dc q) -> p dc q", q=128)[:, :, :]),
                    waits=[("pe", cc)], inc="v")
                bg.set_reader("PB", ro, 512, "v", vv)
        v_x2t_all = em.val("v")
        v_cast_done = 0
        for dc in range(16):
            v_cast_done = em.op("vector", lambda v, d=dc: v.tensor_copy(
                x2T_bf[:, d * 1024:(d + 1) * 1024], x2Tf[:, d * 1024:(d + 1) * 1024]),
                waits=([("pe", pe_wo_done), ("dbg", dbg_aoT)] if dc == 0 else ()),
                inc="v")

        pe_router = None
        for tch in range(2):
            ws = bg.write_waits("PA", tch * 512, 512) + [("v", v_x2t_all),
                                                         ("cst", xd_const)]
            for dc in range(16):
                pe_router = em.op("tensor", lambda t, d=dc, t0=tch: t.matmul(
                    PA[0:8, t0 * 512:(t0 + 1) * 512],
                    rw_sb[:, d * 8:(d + 1) * 8],
                    x2Tf[:, d * 1024 + t0 * 512: d * 1024 + t0 * 512 + 512],
                    start=(d == 0), stop=(d == 15)),
                    waits=ws if dc == 0 else (), inc="pe" if dc == 15 else None)
        vv = em.op("vector", lambda v: v.tensor_copy(lg_sb[:, :], PA[0:8, 0:1024]),
                   waits=[("pe", pe_router), ("s", s_ln1_done)], inc="v")
        bg.set_reader("PA", 0, 1024, "v", vv)
        ws = bg.write_waits("PB", 0, 64) + [("v", vv)]
        cc = None
        for tt in range(8):
            cc = em.op("tensor", lambda t, t0=tt: t.transpose(
                PB[:, t0 * 8:(t0 + 1) * 8], lg_sb[0:8, t0 * 128:(t0 + 1) * 128],
                eye_f[0:8, 0:8]), waits=ws if tt == 0 else (),
                inc="pe" if tt == 7 else None)
        vv = em.op("vector", lambda v: v.tensor_copy(lgT_sb[:, :], PB[:, 0:64]),
                   waits=[("pe", cc)], inc="v")
        bg.set_reader("PB", 0, 64, "v", vv)
        s_rt_prev = {}
        for tt in range(8):
            lsl = lgT_sb[:, tt * 8:(tt + 1) * 8]
            psl = probs_sb[:, tt * 8:(tt + 1) * 8]
            pcol = 64 + (tt % 2) * 8
            vm = em.op("vector", lambda v, l=lsl, pc=pcol: v.tensor_reduce(
                stats[:, pc:pc + 1], l, AX.X, ALU.max, negate=True),
                waits=[("s", s_rt_prev.get(tt % 2, 0))], inc="v")
            sc = em.op("scalar", lambda s_, l=lsl, p=psl, pc=pcol: s_.activation(
                p, l, AF.Exp, bias=stats[:, pc:pc + 1]), waits=[("v", vm)], inc="s")
            s_rt_prev[tt % 2] = sc
            vv = em.op("vector", lambda v, p=psl, pc=pcol: v.tensor_reduce(
                stats[:, pc + 1:pc + 2], p, AX.X, ALU.add), waits=[("s", sc)], inc="v")
            sc2 = em.op("scalar", lambda s_, pc=pcol: s_.activation(
                stats[:, pc + 4:pc + 5], stats[:, pc + 1:pc + 2], AF.Copy),
                waits=[("v", vv)], inc="s")
            vv = em.op("vector", lambda v, pc=pcol: v.reciprocal(
                stats[:, pc + 2:pc + 3], stats[:, pc + 4:pc + 5]),
                waits=[("s", sc2)], inc="v")
            sc2 = em.op("scalar", lambda s_, pc=pcol: s_.activation(
                stats[:, pc + 5:pc + 6], stats[:, pc + 2:pc + 3], AF.Copy),
                waits=[("v", vv)], inc="s")
            em.op("vector", lambda v, p=psl, pc=pcol: v.tensor_scalar_mul(
                p, p, stats[:, pc + 5:pc + 6]), waits=[("s", sc2)])
        vv = em.op("vector", lambda v: v.engine_nop(), inc="v")
        sc = em.op("scalar", lambda s_: s_.activation(
            probs2[:, :], probs_sb[:, :], AF.Copy), waits=[("v", vv)], inc="s")
        vv = 0
        for tt in range(8):
            vv = em.op("vector", lambda v, t0=tt: v.tensor_tensor(
                keys_sb[:, t0 * 8:(t0 + 1) * 8], probs2[:, t0 * 8:(t0 + 1) * 8],
                cons[:, 8:16], op=ALU.subtract),
                waits=[("s", sc)] if tt == 0 else (), inc="v")
        sc = em.op("scalar", lambda s_: s_.activation(
            keys2[:, :], keys_sb[:, 0:64], AF.Copy), waits=[("v", vv)], inc="s")
        v_w_done = 0
        for grp in range(2):
            for lt in range(4):
                tt = grp * 4 + lt
                for e in range(NE):
                    em.op("vector", lambda v, t0=tt, l0=lt, e0=e: v.scalar_tensor_tensor(
                        pm_scr[:, l0 * 64 + e0 * 8: l0 * 64 + (e0 + 1) * 8],
                        keys2[:, t0 * 8:(t0 + 1) * 8],
                        keys2[:, t0 * 8 + e0: t0 * 8 + e0 + 1],
                        probs2[:, t0 * 8:(t0 + 1) * 8],
                        op0=ALU.is_gt, op1=ALU.mult),
                        waits=[("s", sc)] if (lt == 0 and e == 0) else ())
            vv = em.op("vector", lambda v: v.engine_nop(), inc="v")
            sc3 = em.op("scalar", lambda s_: s_.activation(
                pm2[:, :], pm_scr[:, :], AF.Copy), waits=[("v", vv)], inc="s")
            for lt in range(4):
                for e in range(NE):
                    em.op("vector", lambda v, l0=lt, e0=e: v.tensor_reduce(
                        S_all[:, l0 * 8 + e0: l0 * 8 + e0 + 1],
                        pm2[:, l0 * 64 + e0 * 8: l0 * 64 + (e0 + 1) * 8],
                        AX.X, ALU.add),
                        waits=[("s", sc3)] if (lt == 0 and e == 0) else ())
            vv = em.op("vector", lambda v: v.engine_nop(), inc="v")
            sc3 = em.op("scalar", lambda s_: s_.activation(
                S_all2[:, :], S_all[:, :], AF.Copy), waits=[("v", vv)], inc="s")
            vv = em.op("vector", lambda v: v.tensor_single_scalar(
                km_all[:, :], S_all2[:, :], 0.9, ALU.is_le),
                waits=[("s", sc3)], inc="v")
            sc3 = em.op("scalar", lambda s_: s_.activation(
                km2[:, :], km_all[:, :], AF.Copy), waits=[("v", vv)], inc="s")
            for lt in range(4):
                tt = grp * 4 + lt
                v_w_done = em.op("vector", lambda v, t0=tt, l0=lt: v.tensor_tensor(
                    w_sb[:, t0 * 8:(t0 + 1) * 8], probs2[:, t0 * 8:(t0 + 1) * 8],
                    km2[:, l0 * 8:(l0 + 1) * 8], op=ALU.mult),
                    waits=[("s", sc3)] if lt == 0 else (), inc="v")
        gu_dma = {}
        gu_last_mm = {}
        dn_dma = {}
        dn_last_mm = {}
        silu_prev = {}
        hid_last_v = 0
        pe_dn_prev = pe_router
        for e in range(NE):
            for ig in range(32):
                gsl = ig % 2
                gu_dma[("g", e, ig)] = em.op("gpsimd", lambda g, e0=e, i0=ig, s0=gsl: g.dma_start(
                    g_ring.rearrange("p (s t c) -> p s t c", s=2, c=128)[:, s0, :, :],
                    gw_v[e0][:, :, i0 * 128:(i0 + 1) * 128]),
                    waits=[("pe", gu_last_mm.get(gsl, 0))], inc="gr%d" % gsl, amt=16)
                gu_dma[("u", e, ig)] = em.op("gpsimd", lambda g, e0=e, i0=ig, s0=gsl: g.dma_start(
                    u_ring.rearrange("p (s t c) -> p s t c", s=2, c=128)[:, s0, :, :],
                    uw_v[e0][:, :, i0 * 128:(i0 + 1) * 128]),
                    waits=[("pe", gu_last_mm.get(gsl, 0))], inc="ur%d" % gsl, amt=16)
                for th in range(2):
                    gpo = th * 512
                    upo = 1024 + th * 512
                    ws = (bg.write_waits("PA", gpo, 512) + bg.write_waits("PA", upo, 512)
                          + [("gr%d" % gsl, gu_dma[("g", e, ig)]),
                             ("ur%d" % gsl, gu_dma[("u", e, ig)]),
                             ("v", v_cast_done)])
                    cu = None
                    for dc in range(16):
                        em.op("tensor", lambda t, d=dc, s0=gsl, t0=th, o=gpo: t.matmul(
                            PA[:, o:o + 512],
                            g_ring[:, s0 * 2048 + d * 128: s0 * 2048 + (d + 1) * 128],
                            x2T_bf[:, d * 1024 + t0 * 512: d * 1024 + t0 * 512 + 512],
                            start=(d == 0), stop=(d == 15)),
                            waits=ws if dc == 0 else ())
                        cu = em.op("tensor", lambda t, d=dc, s0=gsl, t0=th, o=upo: t.matmul(
                            PA[:, o:o + 512],
                            u_ring[:, s0 * 2048 + d * 128: s0 * 2048 + (d + 1) * 128],
                            x2T_bf[:, d * 1024 + t0 * 512: d * 1024 + t0 * 512 + 512],
                            start=(d == 0), stop=(d == 15)),
                            inc="pe" if dc == 15 else None)
                    gu_last_mm[gsl] = cu
                    sc = em.op("scalar", lambda s_, o=gpo, s0=th: s_.activation(
                        silu_scr[:, s0 * 512:(s0 + 1) * 512], PA[:, o:o + 512], AF.Silu),
                        waits=[("pe", cu), ("v", silu_prev.get(th, 0))], inc="s")
                    hid_last_v = em.op("vector", lambda v, i0=ig, t0=th, o=upo, s0=th: v.tensor_tensor(
                        hidT[:, i0 * 1024 + t0 * 512: i0 * 1024 + t0 * 512 + 512],
                        silu_scr[:, s0 * 512:(s0 + 1) * 512], PA[:, o:o + 512],
                        op=ALU.mult),
                        waits=[("s", sc), ("pe", pe_dn_prev)], inc="v")
                    silu_prev[th] = hid_last_v
                    bg.set_reader("PA", gpo, 512, "v", hid_last_v)
                    bg.set_reader("PA", upo, 512, "v", hid_last_v)
            for hg in range(8):
                pd, pdn = (PB, "PB") if hg % 2 == 0 else (PA, "PA")
                for q4 in range(4):
                    slot = (hg * 4 + q4) % 4
                    ws = [("pe", dn_last_mm.get(slot, 0))]
                    if e == 0 and hg == 0:
                        ws += [("pe", pe_a3_done), ("v", v_a3_done)]
                    dn_dma[(e, hg, q4)] = em.op("gpsimd", lambda g, e0=e, h0=hg, f0=q4, s0=slot: g.dma_start(
                        dn_ring.rearrange("p (s t c) -> p s t c", s=4, c=256)[:, s0, :, :],
                        dw_v[e0][:, f0 * 8:(f0 + 1) * 8, h0 * 256:(h0 + 1) * 256]),
                        waits=ws, inc="dr%d" % slot, amt=16)
                ws0 = [("v", hid_last_v)]
                for b4 in range(4):
                    ws0 += bg.write_waits(pdn, b4 * 512, 512)
                cc_last = None
                for ic in range(32):
                    q4 = ic // 8
                    slot = (hg * 4 + q4) % 4
                    it8 = ic % 8
                    for tq in range(NQT):
                        wlist = []
                        if it8 == 0 and tq == 0:
                            wlist.append(("dr%d" % slot, dn_dma[(e, hg, q4)]))
                            if ic == 0:
                                wlist += ws0
                        do_inc = (ic == 31) or (it8 == 7 and tq == NQT - 1)
                        cc = em.op("tensor", lambda t, i0=ic, t0=tq, s0=slot, i1=it8, p=pd: t.matmul(
                            p[:, t0 * 256:(t0 + 1) * 256],
                            hidT[:, i0 * 1024 + t0 * 128: i0 * 1024 + (t0 + 1) * 128],
                            dn_ring[:, s0 * 2048 + i1 * 256: s0 * 2048 + (i1 + 1) * 256],
                            start=(i0 == 0 and t0 % 2 == 0), stop=(i0 == 31)),
                            waits=wlist, inc="pe" if do_inc else None)
                        if cc is not None:
                            cc_last = cc
                    if it8 == 7:
                        dn_last_mm[slot] = cc_last
                for tq in range(NQT):
                    vv = em.op("vector", lambda v, t0=tq, h0=hg, e0=e, p=pd: v.scalar_tensor_tensor(
                        acc[:, t0 * 2048 + h0 * 256: t0 * 2048 + (h0 + 1) * 256],
                        p[:, t0 * 256:(t0 + 1) * 256],
                        w_sb[:, t0 * 8 + e0: t0 * 8 + e0 + 1],
                        acc[:, t0 * 2048 + h0 * 256: t0 * 2048 + (h0 + 1) * 256],
                        op0=ALU.mult, op1=ALU.add),
                        waits=[("pe", cc_last), ("v", v_w_done), ("dbg", dbg_x2)],
                        inc="v")
                    bg.set_reader(pdn, tq * 256, 256, "v", vv)
            pe_dn_prev = em.val("pe")

        v_dn_all = em.val("v")
        dbg_hid = tap("hid7", hidT[:, :], [128, 32 * 1024], BF16, [("v", v_dn_all)])
        dbg_acc3 = tap("acc3", acc[:, :], [128, 8 * 2048], F32, [("v", v_dn_all)])
        for tq in range(NQT):
            at = acc[:, tq * 2048:(tq + 1) * 2048]
            pcol = 80 + (tq % 2) * 8
            vv = em.op("vector", lambda v, a=at, pc=pcol: v.tensor_reduce(
                stats[:, pc:pc + 1], a, AX.X, ALU.add, negate=True),
                waits=[("v", v_dn_all), ("dbg", dbg_acc3)], inc="v")
            sm = em.op("scalar", lambda s_, pc=pcol: s_.activation(
                stats[:, pc + 1:pc + 2], stats[:, pc:pc + 1], AF.Copy,
                scale=1.0 / D), waits=[("v", vv)], inc="s")
            vv = em.op("vector", lambda v, a=at, pc=pcol: v.tensor_scalar_add(
                a, a, stats[:, pc + 1:pc + 2]), waits=[("s", sm)], inc="v")
            s1 = em.op("scalar", lambda s_, a=at,
                       q=(sq_scr if tq % 2 == 0 else sq_scr2): s_.activation(
                q[:, 0:2048], a, AF.Square), waits=[("v", vv)], inc="s")
            vr = em.op("vector", lambda v, pc=pcol,
                       q=(sq_scr if tq % 2 == 0 else sq_scr2): v.tensor_reduce(
                stats[:, pc + 2:pc + 3], q[:, 0:2048], AX.X, ALU.add),
                waits=[("s", s1)], inc="v")
            em.op("scalar", lambda s_, pc=pcol: s_.activation(
                stats[:, pc + 3:pc + 4], stats[:, pc + 2:pc + 3], AF.Ln,
                scale=1.0 / D, bias=cons[:, 1:2]), waits=[("v", vr)])
            sc = em.op("scalar", lambda s_, pc=pcol: s_.activation(
                stats[:, pc + 4:pc + 5], stats[:, pc + 3:pc + 4], AF.Exp, scale=-0.5),
                inc="s")
            vv = em.op("vector", lambda v, a=at, pc=pcol: v.tensor_scalar_mul(
                a, a, stats[:, pc + 4:pc + 5]), waits=[("s", sc)], inc="v")
            em.op("sync", lambda sy, t=tq, a=at: sy.dma_start(out_t[t], a),
                  waits=[("v", vv)], inc="st", amt=16)
        tap("eyebf_end", eye_bf[:, :], [128, 128], BF16, [("st", 16 * NQT)])
        tap("eyef_end", eye_f[:, :], [128, 128], F32, [("st", 16 * NQT)])
        tap("stats_end", stats[:, :], [128, 128], F32, [("st", 16 * NQT)])
        tap("lnbc_end", ln_bc[:, :], [128, 2048], F32, [("st", 16 * NQT)])
        em.op("sync", lambda sy: sy.nop(),
              waits=[("st", 16 * NQT), ("dbg", em.val("dbg"))])

        with nc.Block() as block:
            @block.tensor
            def _(t):
                em.replay(t, "tensor", semh)

            @block.vector
            def _(v):
                em.replay(v, "vector", semh)

            @block.scalar
            def _(s_):
                em.replay(s_, "scalar", semh)

            @block.gpsimd
            def _(g):
                em.replay(g, "gpsimd", semh)

            @block.sync
            def _(sy):
                em.replay(sy, "sync", semh)

    return nc


_PROG_CACHE = {}


def _get_prog():
    if "nc" not in _PROG_CACHE:
        _PROG_CACHE["nc"] = build_program()
    return _PROG_CACHE["nc"]


def _host_inputs(L0_attn_x, L0_attn_ln_w, L0_attn_wq, L0_attn_wk, L0_attn_wv,
                 L0_attn_wo, L0_ffn_router_w, L0_ffn_gate_w, L0_ffn_up_w,
                 L0_ffn_down_w):
    bf = ml_dtypes.bfloat16

    def perm_cols(nheads):
        p = []
        for h in range(nheads):
            base = h * HD
            p += [base + 2 * i for i in range(64)] + [base + 2 * i + 1 for i in range(64)]
        return np.array(p)

    wq_p = np.ascontiguousarray(np.asarray(L0_attn_wq, np.float32)[:, perm_cols(H)])
    wk_p = np.ascontiguousarray(np.asarray(L0_attn_wk, np.float32)[:, perm_cols(KH)])
    wv_c = np.ascontiguousarray(np.asarray(L0_attn_wv, np.float32))
    wo_c = np.ascontiguousarray(np.asarray(L0_attn_wo, np.float32))
    rwT = np.ascontiguousarray(np.asarray(L0_ffn_router_w, np.float32).T)
    gwT = np.ascontiguousarray(np.asarray(L0_ffn_gate_w, np.float32).transpose(0, 2, 1))
    uwT = np.ascontiguousarray(np.asarray(L0_ffn_up_w, np.float32).transpose(0, 2, 1))
    dwT = np.ascontiguousarray(np.asarray(L0_ffn_down_w, np.float32).transpose(0, 2, 1))
    lnbc = np.ascontiguousarray(np.broadcast_to(
        np.asarray(L0_attn_ln_w, np.float32)[None, :], (128, D)))
    eyef = np.eye(128, dtype=np.float32)
    eyebf = np.eye(128, dtype=np.float32).astype(bf)
    cons = np.zeros((128, 16), dtype=np.float32)
    cons[:, 0] = EPS_RMS
    cons[:, 1] = EPS_LN
    cons[:, 8:16] = np.arange(8, dtype=np.float32)[None, :] * 1e-7

    x = np.asarray(L0_attn_x, np.float32)
    inv_freq = (1.0 / 10000.0 ** (np.arange(0, HD, 2, dtype=np.float32) / HD))

    def tables(kv0):
        pos = np.maximum(kv0 + np.arange(TKV), 0).astype(np.float32)
        ang = pos[:, None] * inv_freq[None, :]
        c = np.ascontiguousarray(np.tile(np.cos(ang), (1, 4)).astype(bf))
        s = np.ascontiguousarray(np.tile(np.sin(ang), (1, 4)).astype(bf))
        return c, s

    def mk_masks(kv0):
        mk = np.full((8, 128, 640), NEGINF, dtype=np.float32)
        ql = np.arange(128)[:, None]
        t = np.arange(640)[None, :]
        band = (t >= ql) & (t <= ql + WIN)
        for j in range(2):
            for qc in range(4):
                kabs = kv0 + j * 512 + qc * 128 + t
                valid = band & (kabs >= 0)
                mk[j * 4 + qc][valid] = 0.0
        return np.ascontiguousarray(mk.astype(bf))

    tabs = {kv0: tables(kv0) for kv0 in (-512, 512)}
    mks = {kv0: mk_masks(kv0) for kv0 in (-512, 512)}

    in_maps = []
    for c in range(8):
        b, m = c // 2, c % 2
        kv0 = -512 if m == 0 else 512
        if m == 0:
            xkv_c = np.concatenate([np.zeros((512, D), np.float32), x[b, 0:1024]], 0)
        else:
            xkv_c = x[b, 512:2048]
        cos_t, sin_t = tabs[kv0]
        in_maps.append({
            "xkv": np.ascontiguousarray(xkv_c), "lnbc": lnbc,
            "wq": wq_p, "wk": wk_p, "wv": wv_c, "wo": wo_c,
            "rwT": rwT, "gwT": gwT, "uwT": uwT, "dwT": dwT,
            "cos4": cos_t, "sin4": sin_t, "masks": mks[kv0],
            "eyef": eyef, "eyebf": eyebf, "cons": cons,
        })
    return in_maps


def kernel(**inputs):
    nc = _get_prog()
    in_maps = _host_inputs(**inputs)
    res = run_bass_kernel_spmd(nc, in_maps, core_ids=list(range(8)))
    out = np.empty((4, 2048, D), dtype=np.float32)
    for c in range(8):
        b, m = c // 2, c % 2
        out[b, m * 1024:(m + 1) * 1024] = res.results[c]["out"]
    return out


# version 06
# speedup vs baseline: 1.0057x; 1.0057x over previous
import numpy as np
import ml_dtypes
import concourse.bass as bass
import concourse.mybir as mybir
from concourse.bass_utils import run_bass_kernel_spmd

F32 = mybir.dt.float32
BF16 = mybir.dt.bfloat16
AF = mybir.ActivationFunctionType
ALU = mybir.AluOpType
AX = mybir.AxisListType

D = 2048
HD = 128
H = 16
KH = 4
WIN = 512
TKV = 1536
TQ = 1024
NKT = TKV // 128
NQT = TQ // 128
II = 4096
NE = 8
LAM = float(1.0 / np.sqrt(128.0))
NEGINF = -1.0e30
EPS_RMS = 1e-6
EPS_LN = 1e-5

ENGINES = ("tensor", "vector", "scalar", "gpsimd", "sync")


class Emit:

    def __init__(self, sems):
        self.q = {e: [] for e in ENGINES}
        self.cnt = {s: 0 for s in sems}

    def op(self, eng, fn, waits=(), inc=None, amt=1):
        w = {}
        for s, v in waits:
            if v and v > 0:
                w[s] = max(w.get(s, 0), v)
        self.q[eng].append((fn, tuple(w.items()), inc, amt))
        if inc is not None:
            self.cnt[inc] += amt
            return self.cnt[inc]
        return None

    def val(self, sem):
        return self.cnt[sem]

    def replay(self, engobj, eng, semh):
        waited = {}
        for fn, waits, inc, amt in self.q[eng]:
            for s, v in waits:
                if waited.get(s, 0) < v:
                    engobj.wait_ge(semh[s], v)
                    waited[s] = v
            ins = fn(engobj)
            if inc is not None:
                ins.then_inc(semh[inc], amt)


class BankGuard:

    def __init__(self):
        self.last = {}

    @staticmethod
    def _banks(tname, off_b, size_b):
        return [(tname, b) for b in range(off_b // 2048, (off_b + size_b - 1) // 2048 + 1)]

    def write_waits(self, tname, off_el, n_el):
        out = []
        for k in self._banks(tname, off_el * 4, n_el * 4):
            if k in self.last:
                out.append(self.last[k])
        return out

    def set_reader(self, tname, off_el, n_el, sem, val):
        for k in self._banks(tname, off_el * 4, n_el * 4):
            self.last[k] = (sem, val)


def build_program(debug=False):
    nc = bass.Bass("TRN2", target_bir_lowering=False)

    xkv = nc.dram_tensor("xkv", [TKV, D], F32, kind="ExternalInput").ap()
    lnbc = nc.dram_tensor("lnbc", [128, D], F32, kind="ExternalInput").ap()
    wq = nc.dram_tensor("wq", [D, D], F32, kind="ExternalInput").ap()
    wk = nc.dram_tensor("wk", [D, KH * HD], F32, kind="ExternalInput").ap()
    wv = nc.dram_tensor("wv", [D, KH * HD], F32, kind="ExternalInput").ap()
    wo = nc.dram_tensor("wo", [D, D], F32, kind="ExternalInput").ap()
    rwT = nc.dram_tensor("rwT", [D, NE], F32, kind="ExternalInput").ap()
    gwT = nc.dram_tensor("gwT", [NE, D, II], F32, kind="ExternalInput").ap()
    uwT = nc.dram_tensor("uwT", [NE, D, II], F32, kind="ExternalInput").ap()
    dwT = nc.dram_tensor("dwT", [NE, II, D], F32, kind="ExternalInput").ap()
    cos4 = nc.dram_tensor("cos4", [TKV, 256], BF16, kind="ExternalInput").ap()
    sin4 = nc.dram_tensor("sin4", [TKV, 256], BF16, kind="ExternalInput").ap()
    masks = nc.dram_tensor("masks", [8, 128, 640], BF16, kind="ExternalInput").ap()
    eyef_in = nc.dram_tensor("eyef", [128, 128], F32, kind="ExternalInput").ap()
    eyebf_in = nc.dram_tensor("eyebf", [128, 128], BF16, kind="ExternalInput").ap()
    cons_in = nc.dram_tensor("cons", [128, 16], F32, kind="ExternalInput").ap()
    out = nc.dram_tensor("out", [TQ, D], F32, kind="ExternalOutput").ap()

    xkv_t = xkv.rearrange("(t p) c -> t p c", p=128)
    wq_v = wq.rearrange("(t p) c -> p t c", p=128)
    wk_v = wk.rearrange("(t p) c -> p t c", p=128)
    wv_v = wv.rearrange("(t p) c -> p t c", p=128)
    wo_v = wo.rearrange("(t p) c -> p t c", p=128)
    rw_v = rwT.rearrange("(t p) e -> p t e", p=128)
    gw_v = gwT.rearrange("e (t p) c -> e p t c", p=128)
    uw_v = uwT.rearrange("e (t p) c -> e p t c", p=128)
    dw_v = dwT.rearrange("e (t p) c -> e p t c", p=128)
    cos_v = cos4.rearrange("(t p) c -> p t c", p=128)
    sin_v = sin4.rearrange("(t p) c -> p t c", p=128)
    mask_v = masks.rearrange("j p t -> p j t")
    out_t = out.rearrange("(t p) c -> t p c", p=128)

    ARENA_SIZE = 225280 - 16384
    arena = nc.alloc_sbuf_tensor("manual_arena", [128, ARENA_SIZE],
                                 mybir.dt.uint8)
    A0 = nc.lookup_mloc(arena).addr
    assert A0 + ARENA_SIZE <= 228000, A0
    sb = {}
    off = [A0]

    def alloc(name, shape, dt, at=None):
        nbytes = int(np.prod(shape[1:])) * (2 if dt == BF16 else 4)
        o = off[0] if at is None else at
        if at is None:
            off[0] = o + ((nbytes + 63) // 64) * 64
        h = nc.alloc_sbuf_tensor_at(name, list(shape), dt, offset=o)
        sb[name] = h.ap()
        return sb[name]

    eye_f = alloc("eye_f", [128, 128], F32)
    eye_bf = alloc("eye_bf", [128, 128], BF16)
    cons = alloc("cons_sb", [128, 16], F32)
    rw_sb = alloc("rw_sb", [128, 128], F32)
    stats = alloc("stats", [128, 128], F32)
    w_sb = alloc("w_sb", [128, 64], F32)
    probs_sb = alloc("probs_sb", [128, 64], F32)
    lgT_sb = alloc("lgT_sb", [128, 64], F32)
    S_sb = alloc("S_sb", [128, 16], F32)
    scr8 = alloc("scr8", [128, 16], F32)
    keys_sb = alloc("keys_sb", [128, 64], F32)
    probs2 = alloc("probs2", [128, 64], F32)
    keys2 = alloc("keys2", [128, 64], F32)
    pm_scr = alloc("pm_scr", [128, 256], F32)
    pm2 = alloc("pm2", [128, 256], F32)
    S_all = alloc("S_all", [128, 32], F32)
    S_all2 = alloc("S_all2", [128, 32], F32)
    km_all = alloc("km_all", [128, 32], F32)
    km2 = alloc("km2", [128, 32], F32)
    tmp1 = alloc("tmp1", [128, 128], F32)
    tmp2 = alloc("tmp2", [128, 128], F32)
    silu_scr = alloc("silu_scr", [128, 1024], F32)
    lg_sb = alloc("lg_sb", [8, 1024], F32, at=off[0] - 4096)
    dbg_r_sb = alloc("dbg_r_sb", [128, 16], F32)
    g_ring = alloc("g_ring", [128, 2 * 2048], BF16)
    u_ring = alloc("u_ring", [128, 2 * 2048], BF16)
    E0 = ((off[0] + 511) // 512) * 512

    xnT = alloc("xnT", [128, 16 * TKV], BF16, at=E0)
    aoT = alloc("aoT", [128, 16 * 1024], BF16, at=E0)
    x2T_bf = alloc("x2T_bf", [128, 16 * 1024], BF16, at=E0)
    probsT = alloc("probsT", [128, 2 * 8 * 512], BF16, at=E0 + 32768)
    dn_ring = alloc("dn_ring", [128, 4 * 2048], BF16, at=E0 + 32768)
    x_ring = alloc("x_ring", [128, 2 * 2048], F32, at=E0 + 49152)
    xn_ring = alloc("xn_ring", [128, 2 * 2048], BF16, at=E0 + 65536)
    ln_bc = alloc("ln_bc", [128, 2048], F32, at=E0 + 73728)
    wq_ring = alloc("wq_ring", [128, 3 * 4096], BF16, at=E0 + 49152)
    wo_ring = alloc("wo_ring", [128, 3 * 4096], BF16, at=E0 + 49152)
    cos_sb = alloc("cos_sb", [128, 12 * 256], BF16, at=E0 + 73728)
    sin_sb = alloc("sin_sb", [128, 12 * 256], BF16, at=E0 + 79872)
    sq_scr2 = alloc("sq_scr2", [128, 2048], BF16, at=E0 + 81920)
    sq_scr = alloc("sq_scr", [128, 2048], BF16, at=E0 + 87040)
    xcol_ring = alloc("xcol_ring", [128, 4 * 256], F32, at=E0 + 73728)
    qk_ring = alloc("qk_ring", [128, 2 * 256], BF16, at=E0 + 86016)
    mask_sb = alloc("mask_sb", [128, 8 * 640], BF16, at=E0 + 87040)
    ms_sb = alloc("ms_sb", [128, 2 * 640], F32, at=E0 + 100352)
    ex_sb = alloc("ex_sb", [128, 2 * 640], F32, at=E0 + 105472)
    pb_sb = alloc("pb_sb", [128, 2 * 640], BF16, at=E0 + 110592)
    x2Tf = alloc("x2Tf", [128, 16 * 1024], F32, at=E0 + 49152)
    hidT = alloc("hidT", [128, 32 * 1024], BF16, at=E0 + 49152)
    qT = alloc("qT", [128, 16 * 1024], BF16, at=E0 + 114688)
    kT = alloc("kT", [128, 4 * TKV], BF16, at=E0 + 147456)
    v_sb = alloc("v_sb", [128, 12 * 512], BF16, at=E0 + 159744)
    acc = alloc("acc", [128, 8 * 2048], F32, at=E0 + 114688)
    assert E0 + 114688 + 8 * 2048 * 4 <= 229300, E0

    sems = (["pe", "v", "s", "st", "dbg", "cst", "msk"]
            + ["xr%d" % i for i in range(2)] + ["xc%d" % i for i in range(4)]
            + ["wr%d" % i for i in range(3)] + ["gr%d" % i for i in range(2)]
            + ["ur%d" % i for i in range(2)] + ["dr%d" % i for i in range(4)])
    em = Emit(sems)
    taps = {}

    def tap(name, ap, shape, dt, waits):
        if not debug:
            return 0
        t = nc.dram_tensor("dbg_" + name, list(shape), dt, kind="ExternalOutput").ap()
        return em.op("sync", lambda sy, d=t, s_=ap: sy.dma_start(d, s_),
                     waits=waits, inc="dbg", amt=16)
    bg = BankGuard()
    slot_order = [0, 2, 4, 6, 1, 3, 5, 7]

    import contextlib
    _es = contextlib.ExitStack()
    with _es:
        PAh = _es.enter_context(nc.psum_tensor("PA", [128, 2048], F32))
        PBh = _es.enter_context(nc.psum_tensor("PB", [128, 2048], F32))
        semh = {name: _es.enter_context(nc.semaphore("sem_" + name))
                for name in sems}
        PA = PAh[:, :]
        PB = PBh[:, :]
        PT = [PB[:, 1024:1536].bitcast(BF16), PB[:, 1536:2048].bitcast(BF16)]

        xd_const = 0
        for dst, src in (
            (eye_f, eyef_in), (eye_bf, eyebf_in), (cons, cons_in), (ln_bc, lnbc),
            (rw_sb.rearrange("p (t e) -> p t e", e=8), rw_v),
        ):
            xd_const = em.op("sync", lambda sy, d=dst, s_=src: sy.dma_start(d, s_),
                             inc="cst", amt=16)

        v_stt = [0] * NKT
        s_rms = [0] * NKT
        s_sq = [0] * NKT
        v_red = [0] * NKT
        dbg_t0 = [0]
        pe_xnt = [0] * NKT
        xd_x = [0] * NKT
        for tk in range(NKT):
            sl = tk % 2
            ws = []
            if tk >= 2:
                ws = [("v", v_stt[tk - 2]), ("s", s_rms[tk - 2])]
            xd_x[tk] = em.op("sync", lambda sy, t=tk, s0=sl: sy.dma_start(
                x_ring[:, s0 * 2048:(s0 + 1) * 2048], xkv_t[t]), waits=ws,
                inc="xr%d" % sl, amt=16)
            pcol = (tk % 2) * 8
            ws = [("xr%d" % sl, xd_x[tk])]
            if tk >= 2:
                ws.append(("v", v_red[tk - 2]))
                if debug and tk == 2:
                    ws.append(("dbg", dbg_t0[0]))
            s_sq[tk] = em.op("scalar", lambda s_, s0=sl, pc=pcol,
                             q=(sq_scr if sl == 0 else sq_scr2): s_.activation(
                q[:, 0:2048], x_ring[:, s0 * 2048:(s0 + 1) * 2048], AF.Square),
                waits=ws, inc="s")
            ws = [("s", s_sq[tk])]
            if tk >= 2:
                ws.append(("s", s_rms[tk - 2]))
            v_red[tk] = em.op("vector", lambda v, pc=pcol,
                              q=(sq_scr if sl == 0 else sq_scr2): v.tensor_reduce(
                stats[:, pc:pc + 1], q[:, 0:2048], AX.X, ALU.add), waits=ws, inc="v")
            em.op("scalar", lambda s_, pc=pcol: s_.activation(
                stats[:, pc + 1:pc + 2], stats[:, pc:pc + 1], AF.Ln, scale=1.0 / D,
                bias=cons[:, 0:1]), waits=[("v", v_red[tk])])
            s_rms[tk] = em.op("scalar", lambda s_, pc=pcol: s_.activation(
                stats[:, pc + 2:pc + 3], stats[:, pc + 1:pc + 2], AF.Exp, scale=-0.5),
                inc="s")
            if tk == 0:
                dbg_t0[0] = tap("sq0", sq_scr[:, :], [128, 2048], BF16,
                                [("s", s_sq[0]), ("v", v_red[0])])
                dbg_t0[0] = max(dbg_t0[0], tap("st0", stats[:, 0:16], [128, 16], F32,
                                               [("s", s_rms[0])]))
            if debug:
                em.op("scalar", lambda s_, pc=pcol, t=tk: s_.copy(
                    dbg_r_sb[:, t:t + 1], stats[:, pc + 2:pc + 3]))
            ws = [("s", s_rms[tk]), ("cst", xd_const)]
            if tk >= 2:
                ws.append(("pe", pe_xnt[tk - 2]))
            v_stt[tk] = em.op("vector", lambda v, s0=sl, pc=pcol: v.scalar_tensor_tensor(
                xn_ring[:, s0 * 2048:(s0 + 1) * 2048],
                x_ring[:, s0 * 2048:(s0 + 1) * 2048],
                stats[:, pc + 2:pc + 3], ln_bc[:, :], op0=ALU.mult, op1=ALU.mult),
                waits=ws, inc="v")
            for half in range(2):
                pt = PT[half]
                pto = 1024 + half * 512
                ws = bg.write_waits("PB", pto, 512) + [("v", v_stt[tk])]
                cc = None
                for dc8 in range(8):
                    dc = half * 8 + dc8
                    cc = em.op("tensor", lambda t, s0=sl, d=dc, d8=dc8, p=pt: t.transpose(
                        p[:, d8 * 128:(d8 + 1) * 128],
                        xn_ring[:, s0 * 2048 + d * 128: s0 * 2048 + (d + 1) * 128],
                        eye_bf[:, :]),
                        waits=ws if dc8 == 0 else (), inc="pe" if dc8 == 7 else None)
                pe_xnt[tk] = cc
                vv = em.op("vector", lambda v, t=tk, h=half, p=pt: v.tensor_copy(
                    xnT.rearrange("p (dc q) -> p dc q", q=TKV)[:, h * 8:(h + 1) * 8,
                                                              t * 128:(t + 1) * 128],
                    p.rearrange("p (dc q) -> p dc q", q=128)[:, :, :]),
                    waits=[("pe", pe_xnt[tk])], inc="v")
                bg.set_reader("PB", pto, 512, "v", vv)
        v_a1_done = em.val("v")
        s_a1_done = em.val("s")
        pe_a1_done = em.val("pe")
        dbg_xnT = tap("xnT", xnT[:, :], [128, 16 * TKV], BF16, [("v", v_a1_done)])
        tap("r", dbg_r_sb[:, :], [128, 16], F32, [("s", s_a1_done), ("v", v_a1_done)])
        tap("lnbc1", ln_bc[:, :], [128, 2048], F32, [("v", v_a1_done)])

        em.op("sync", lambda sy: sy.dma_start(
            mask_sb.rearrange("p (j t) -> p j t", t=640), mask_v),
            waits=[("v", v_a1_done), ("s", s_a1_done)], inc="msk", amt=16)
        em.op("sync", lambda sy: sy.dma_start(
            cos_sb.rearrange("p (t c) -> p t c", c=256), cos_v),
            waits=[("v", v_a1_done)], inc="msk", amt=16)
        xd_cs = em.op("sync", lambda sy: sy.dma_start(
            sin_sb.rearrange("p (t c) -> p t c", c=256), sin_v), inc="msk", amt=16)

        wq_dma = {}
        wq_last_mm = {}
        qk_last_tr = {}
        proj_i = [0]

        def emit_weight_halfgroup(key, hseq, src_view):
            slot = hseq % 3
            ws = [("pe", wq_last_mm.get(slot, 0))]
            if hseq < 3:
                ws += [("v", v_a1_done), ("s", s_a1_done), ("pe", pe_a1_done)]
            wq_dma[key] = em.op("gpsimd", lambda g, sv=src_view, s0=slot: g.dma_start(
                wq_ring.rearrange("p (s t c) -> p s t c", s=3, c=256)[:, s0, :, :], sv),
                waits=ws, inc="wr%d" % slot, amt=16)
            return slot

        def emit_proj_unit(key, slot, tlist, dest_fn, rope):
            for tt in tlist:
                s8 = slot_order[proj_i[0] % 8]
                po = s8 * 256
                ws = bg.write_waits("PA", po, 256) + [
                    ("wr%d" % slot, wq_dma[key]), ("v", v_a1_done)]
                pe_stop = None
                for dc in range(16):
                    pe_stop = em.op("tensor", lambda t, d=dc, s0=slot, t0=tt, p0=po: t.matmul(
                        PA[:, p0:p0 + 256],
                        xnT[:, d * TKV + t0 * 128: d * TKV + (t0 + 1) * 128],
                        wq_ring[:, s0 * 4096 + d * 256: s0 * 4096 + (d + 1) * 256],
                        start=(d == 0), stop=(d == 15)),
                        waits=ws if dc == 0 else (), inc="pe" if dc == 15 else None)
                wq_last_mm[slot] = pe_stop
                if rope:
                    qsl = proj_i[0] % 2
                    pe_e = PA[:, po:po + 256].rearrange("p (g c) -> p g c", c=128)[:, :, 0:64]
                    pe_o = PA[:, po:po + 256].rearrange("p (g c) -> p g c", c=128)[:, :, 64:128]
                    cs = cos_sb[:, tt * 256: tt * 256 + 128].rearrange("p (g c) -> p g c", c=64)
                    sn = sin_sb[:, tt * 256: tt * 256 + 128].rearrange("p (g c) -> p g c", c=64)
                    dst = qk_ring[:, qsl * 256:(qsl + 1) * 256].rearrange("p (g c) -> p g c", c=128)
                    t1v = tmp1.rearrange("p (g c) -> p g c", c=64)[:, 0:2, :]
                    t2v = tmp2.rearrange("p (g c) -> p g c", c=64)[:, 0:2, :]
                    ws2 = [("pe", pe_stop), ("msk", xd_cs),
                           ("pe", qk_last_tr.get(qsl, 0))]
                    em.op("vector", lambda v, a=pe_e, b=cs, t=t1v: v.tensor_tensor(t, a, b, op=ALU.mult), waits=ws2)
                    em.op("vector", lambda v, a=pe_o, b=sn, t=t2v: v.tensor_tensor(t, a, b, op=ALU.mult))
                    em.op("vector", lambda v, d=dst, t=t1v, u=t2v: v.tensor_tensor(
                        d[:, :, 0:64], t, u, op=ALU.subtract))
                    em.op("vector", lambda v, a=pe_e, b=sn, t=t1v: v.tensor_tensor(t, a, b, op=ALU.mult))
                    em.op("vector", lambda v, a=pe_o, b=cs, t=t2v: v.tensor_tensor(t, a, b, op=ALU.mult))
                    vv = em.op("vector", lambda v, d=dst, t=t1v, u=t2v: v.tensor_tensor(
                        d[:, :, 64:128], t, u, op=ALU.add), inc="v")
                    bg.set_reader("PA", po, 256, "v", vv)
                    half = proj_i[0] % 2
                    pt = PT[half]
                    pto = 1024 + half * 512
                    ws3 = bg.write_waits("PB", pto, 512) + [("v", vv)]
                    cc = None
                    for g2 in range(2):
                        cc = em.op("tensor", lambda t, q=qsl, g=g2, p=pt: t.transpose(
                            p[:, g * 128:(g + 1) * 128],
                            qk_ring[:, q * 256 + g * 128: q * 256 + (g + 1) * 128],
                            eye_bf[:, :]),
                            waits=ws3 if g2 == 0 else (), inc="pe" if g2 == 1 else None)
                    qk_last_tr[qsl] = cc
                    vv2 = em.op("vector", lambda v, p=pt, t0=tt, df=dest_fn: v.tensor_copy(
                        df(t0), p.rearrange("p (g q) -> p g q", q=128)[:, 0:2, :]),
                        waits=[("pe", cc)], inc="v")
                    bg.set_reader("PB", pto, 512, "v", vv2)
                else:
                    vv = em.op("vector", lambda v, t0=tt, p0=po, df=dest_fn: v.tensor_copy(
                        df(t0), PA[:, p0:p0 + 256]), waits=[("pe", pe_stop)], inc="v")
                    bg.set_reader("PA", po, 256, "v", vv)
                proj_i[0] += 1

        qT_v = qT.rearrange("p (h q) -> p h q", q=1024)
        kT_v = kT.rearrange("p (h q) -> p h q", q=TKV)
        for hp in range(8):
            slot = emit_weight_halfgroup(("q", hp), hp, wq_v[:, :, hp * 256:(hp + 1) * 256])
            emit_proj_unit(("q", hp), slot, range(4, 12),
                           (lambda t0, h2=hp * 2: qT_v[:, h2:h2 + 2, (t0 - 4) * 128:(t0 - 3) * 128]),
                           rope=True)
        for kp in range(2):
            slot = emit_weight_halfgroup(("k", kp), 8 + kp, wk_v[:, :, kp * 256:(kp + 1) * 256])
            emit_proj_unit(("k", kp), slot, range(NKT),
                           (lambda t0, h2=kp * 2: kT_v[:, h2:h2 + 2, t0 * 128:(t0 + 1) * 128]),
                           rope=True)
        for kp in range(2):
            slot = emit_weight_halfgroup(("v", kp), 10 + kp, wv_v[:, :, kp * 256:(kp + 1) * 256])
            emit_proj_unit(("v", kp), slot, range(NKT),
                           (lambda t0, k0=kp: v_sb[:, t0 * 512 + k0 * 256: t0 * 512 + k0 * 256 + 256]),
                           rope=False)

        pe_a2_done = em.val("pe")
        v_a2_done = em.val("v")
        dbg_qT = tap("qT", qT[:, :], [128, 16 * 1024], BF16, [("v", v_a2_done)])
        dbg_kT = tap("kT", kT[:, :], [128, 4 * TKV], BF16, [("v", v_a2_done)])
        dbg_v = tap("v_sb", v_sb[:, :], [128, 12 * 512], BF16, [("v", v_a2_done)])

        s_exp_prev = {}
        v_pb_prev = {}
        dbg_h0 = [0]
        pb_last_pe = {}
        probsT_attn_last = {}
        sc_i = [0]
        pT_v = probsT.rearrange("p (b c q) -> p b c q", b=2, q=512)
        pending = []

        def emit_attnv(j, h, v_probsT):
            kh = h // 4
            buf = (j * H + h) % 2
            avo = buf * 512
            ws = bg.write_waits("PB", avo, 512) + [("v", v_probsT)]
            cc = None
            for c8 in range(8):
                lo = max(0, c8 - 4) * 128
                hi = (min(3, c8) + 1) * 128
                vt = j * 4 + c8
                cc = em.op("tensor", lambda t, o=avo, c0=c8, l=lo, h2=hi, v0=vt,
                           k0=kh, b=buf: t.matmul(
                    PB[:, o + l:o + h2],
                    v_sb[:, v0 * 512 + k0 * 128: v0 * 512 + (k0 + 1) * 128],
                    pT_v[:, b, c0, l:h2], start=(c0 == 0), stop=(c0 == 7)),
                    waits=ws if c8 == 0 else (), inc="pe" if c8 == 7 else None)
            probsT_attn_last[buf] = cc
            vv = em.op("vector", lambda v, o=avo, h0=h, j0=j: v.tensor_copy(
                aoT[:, h0 * 1024 + j0 * 512: h0 * 1024 + j0 * 512 + 512],
                PB[:, o:o + 512]), waits=[("pe", cc), ("dbg", dbg_xnT)], inc="v")
            bg.set_reader("PB", avo, 512, "v", vv)

        for j in range(2):
            for h in range(H):
                kh = h // 4
                buf = (j * H + h) % 2
                for qc in range(4):
                    si = sc_i[0] % 2
                    sc_i[0] += 1
                    so = si * 1024
                    pcol = 32 + si * 8
                    ws = bg.write_waits("PA", so, 640) + [("v", v_a2_done)]
                    kbase = kh * TKV + j * 512 + qc * 128
                    qbase = h * 1024 + j * 512 + qc * 128
                    em.op("tensor", lambda t, qb=qbase, kb=kbase, o=so: t.matmul(
                        PA[:, o:o + 512], qT[:, qb:qb + 128], kT[:, kb:kb + 512],
                        start=True, stop=True), waits=ws)
                    pe_sc = em.op("tensor", lambda t, qb=qbase, kb=kbase, o=so: t.matmul(
                        PA[:, o + 512:o + 640], qT[:, qb:qb + 128],
                        kT[:, kb + 512:kb + 640], start=True, stop=True), inc="pe")
                    msl = ms_sb[:, si * 640:(si + 1) * 640]
                    exl = ex_sb[:, si * 640:(si + 1) * 640]
                    pbl = pb_sb[:, si * 640:(si + 1) * 640]
                    mko = (j * 4 + qc) * 640
                    ws = [("pe", pe_sc), ("msk", xd_cs),
                          ("s", s_exp_prev.get(si, 0)), ("pe", pb_last_pe.get(si, 0))]
                    ws.append(("dbg", dbg_h0[0]))
                    em.op("vector", lambda v, m=msl, o=so, mo=mko: v.scalar_tensor_tensor(
                        m, PA[:, o:o + 640], LAM, mask_sb[:, mo:mo + 640],
                        op0=ALU.mult, op1=ALU.add), waits=ws)
                    vv = em.op("vector", lambda v, m=msl, pc=pcol: v.tensor_reduce(
                        stats[:, pc:pc + 1], m, AX.X, ALU.max, negate=True), inc="v")
                    bg.set_reader("PA", so, 640, "v", vv)
                    sc = em.op("scalar", lambda s_, m=msl, e=exl, pc=pcol: s_.activation(
                        e, m, AF.Exp, bias=stats[:, pc:pc + 1]),
                        waits=[("v", vv), ("v", v_pb_prev.get(si, 0))], inc="s")
                    s_exp_prev[si] = sc
                    vv = em.op("vector", lambda v, e=exl, pc=pcol: v.tensor_reduce(
                        stats[:, pc + 2:pc + 3], e, AX.X, ALU.add),
                        waits=[("s", sc)], inc="v")
                    sc2 = em.op("scalar", lambda s_, pc=pcol: s_.activation(
                        stats[:, pc + 4:pc + 5], stats[:, pc + 2:pc + 3], AF.Copy),
                        waits=[("v", vv)], inc="s")
                    vv = em.op("vector", lambda v, pc=pcol: v.reciprocal(
                        stats[:, pc + 3:pc + 4], stats[:, pc + 4:pc + 5]),
                        waits=[("s", sc2)], inc="v")
                    sc2 = em.op("scalar", lambda s_, pc=pcol: s_.activation(
                        stats[:, pc + 5:pc + 6], stats[:, pc + 3:pc + 4], AF.Copy),
                        waits=[("v", vv)], inc="s")
                    vv = em.op("vector", lambda v, e=exl, p=pbl, pc=pcol: v.tensor_scalar_mul(
                        p, e, stats[:, pc + 5:pc + 6]), waits=[("s", sc2)], inc="v")
                    v_pb_prev[si] = vv
                    pt = PT[si]
                    pto = 1024 + si * 512
                    ws = bg.write_waits("PB", pto, 512) + [("v", vv)]
                    cc = None
                    for t5 in range(5):
                        cc = em.op("tensor", lambda t, p=pt, q=t5, pb=pbl: t.transpose(
                            p[:, q * 128:(q + 1) * 128], pb[:, q * 128:(q + 1) * 128],
                            eye_bf[:, :]), waits=ws if t5 == 0 else (),
                            inc="pe" if t5 == 4 else None)
                    pb_last_pe[si] = cc
                    vv2 = em.op("vector", lambda v, b=buf, q=qc, p=pt: v.tensor_copy(
                        pT_v[:, b, q:q + 5, q * 128:(q + 1) * 128],
                        p.rearrange("p (c q) -> p c q", q=128)[:, 0:5, :]),
                        waits=[("pe", cc), ("pe", probsT_attn_last.get(buf, 0)),
                               ("dbg", dbg_xnT)],
                        inc="v")
                    bg.set_reader("PB", pto, 512, "v", vv2)
                v_probsT = em.val("v")
                pending.append((j, h, v_probsT))
                if len(pending) > 1:
                    emit_attnv(*pending.pop(0))
        while pending:
            emit_attnv(*pending.pop(0))
        pe_a3_done = em.val("pe")
        v_a3_done = em.val("v")
        s_a3_done = em.val("s")
        dbg_aoT = tap("aoT", aoT[:, :], [128, 16 * 1024], BF16, [("v", v_a3_done)])
        dbg_qkv = max(dbg_qT, dbg_kT, dbg_v) if debug else 0

        wo_dma = {}
        wo_last_mm = {}
        xc_dma = {}
        v_accw = {}
        wo_i = [0]
        v_ln1 = [0] * NQT
        for hg in range(8):
            slot = hg % 3
            ws = [("pe", wo_last_mm.get(slot, 0))]
            if hg < 3:
                ws += [("pe", pe_a2_done), ("v", v_a2_done)]
            else:
                ws += [("v", v_a3_done), ("s", s_a3_done), ("pe", pe_a3_done)]
            wo_dma[hg] = em.op("gpsimd", lambda g, h0=hg, s0=slot: g.dma_start(
                wo_ring.rearrange("p (s t c) -> p s t c", s=3, c=256)[:, s0, :, :],
                wo_v[:, :, h0 * 256:(h0 + 1) * 256]), waits=ws,
                inc="wr%d" % slot, amt=16)
            for tq in range(NQT):
                seq = hg * 8 + tq
                xsl = tq % 4
                ws = [("v", v_a2_done)] if seq < 4 else [("v", v_accw.get(seq - 4, 0))]
                if seq < 4:
                    ws.append(("pe", pe_a3_done))
                xc_dma[seq] = em.op("sync", lambda sy, t=tq, h0=hg, s0=xsl: sy.dma_start(
                    xcol_ring[:, s0 * 256:(s0 + 1) * 256],
                    xkv_t[4 + t][:, h0 * 256:(h0 + 1) * 256]), waits=ws,
                    inc="xc%d" % xsl, amt=16)
                s8 = slot_order[wo_i[0] % 8]
                wo_i[0] += 1
                po = s8 * 256
                ws = bg.write_waits("PA", po, 256) + [("wr%d" % slot, wo_dma[hg]),
                                                      ("v", v_a3_done)]
                cc = None
                for hc in range(16):
                    cc = em.op("tensor", lambda t, h2=hc, t0=tq, s0=slot, p0=po: t.matmul(
                        PA[:, p0:p0 + 256],
                        aoT[:, h2 * 1024 + t0 * 128: h2 * 1024 + (t0 + 1) * 128],
                        wo_ring[:, s0 * 4096 + h2 * 256: s0 * 4096 + (h2 + 1) * 256],
                        start=(h2 == 0), stop=(h2 == 15)),
                        waits=ws if hc == 0 else (), inc="pe" if hc == 15 else None)
                wo_last_mm[slot] = cc
                vv = em.op("vector", lambda v, t0=tq, h0=hg, s0=xsl, p0=po: v.tensor_tensor(
                    acc[:, t0 * 2048 + h0 * 256: t0 * 2048 + (h0 + 1) * 256],
                    PA[:, p0:p0 + 256], xcol_ring[:, s0 * 256:(s0 + 1) * 256],
                    op=ALU.add), waits=[("pe", cc), ("xc%d" % xsl, xc_dma[seq]),
                                        ("dbg", dbg_qkv)], inc="v")
                bg.set_reader("PA", po, 256, "v", vv)
                v_accw[seq] = vv
        for tq in range(NQT):
            at = acc[:, tq * 2048:(tq + 1) * 2048]
            pcol = 48 + (tq % 2) * 8
            vv = em.op("vector", lambda v, a=at, pc=pcol: v.tensor_reduce(
                stats[:, pc:pc + 1], a, AX.X, ALU.add, negate=True),
                waits=[("v", v_accw[7 * 8 + tq])], inc="v")
            sm = em.op("scalar", lambda s_, pc=pcol: s_.activation(
                stats[:, pc + 1:pc + 2], stats[:, pc:pc + 1], AF.Copy,
                scale=1.0 / D), waits=[("v", vv)], inc="s")
            vv = em.op("vector", lambda v, a=at, pc=pcol: v.tensor_scalar_add(
                a, a, stats[:, pc + 1:pc + 2]), waits=[("s", sm)], inc="v")
            s1 = em.op("scalar", lambda s_, a=at,
                       q=(sq_scr if tq % 2 == 0 else sq_scr2): s_.activation(
                q[:, 0:2048], a, AF.Square), waits=[("v", vv)], inc="s")
            vr = em.op("vector", lambda v, pc=pcol,
                       q=(sq_scr if tq % 2 == 0 else sq_scr2): v.tensor_reduce(
                stats[:, pc + 2:pc + 3], q[:, 0:2048], AX.X, ALU.add),
                waits=[("s", s1)], inc="v")
            em.op("scalar", lambda s_, pc=pcol: s_.activation(
                stats[:, pc + 3:pc + 4], stats[:, pc + 2:pc + 3], AF.Ln,
                scale=1.0 / D, bias=cons[:, 1:2]), waits=[("v", vr)])
            sc = em.op("scalar", lambda s_, pc=pcol: s_.activation(
                stats[:, pc + 4:pc + 5], stats[:, pc + 3:pc + 4], AF.Exp, scale=-0.5),
                inc="s")
            v_ln1[tq] = em.op("vector", lambda v, a=at, pc=pcol: v.tensor_scalar_mul(
                a, a, stats[:, pc + 4:pc + 5]), waits=[("s", sc)], inc="v")
        pe_wo_done = em.val("pe")
        s_ln1_done = em.val("s")
        dbg_x2 = tap("x2", acc[:, :], [128, 8 * 2048], F32, [("v", em.val("v"))])

        x2Tf_v = x2Tf.rearrange("p (dc q) -> p dc q", q=1024)
        for tq in range(NQT):
            for r4 in range(4):
                reg = r4 % 2
                ro = reg * 512
                ws = bg.write_waits("PB", ro, 512) + [("v", v_ln1[tq])]
                cc = None
                for d4 in range(4):
                    dc = r4 * 4 + d4
                    cc = em.op("tensor", lambda t, t0=tq, d=dc, d0=d4, o=ro: t.transpose(
                        PB[:, o + d0 * 128: o + (d0 + 1) * 128],
                        acc[:, t0 * 2048 + d * 128: t0 * 2048 + (d + 1) * 128],
                        eye_f[:, :]), waits=ws if d4 == 0 else (),
                        inc="pe" if d4 == 3 else None)
                vv = em.op("vector", lambda v, t0=tq, r=r4, o=ro: v.tensor_copy(
                    x2Tf_v[:, r * 4:(r + 1) * 4, t0 * 128:(t0 + 1) * 128],
                    PB[:, o:o + 512].rearrange("p (dc q) -> p dc q", q=128)[:, :, :]),
                    waits=[("pe", cc), ("pe", pe_wo_done)], inc="v")
                bg.set_reader("PB", ro, 512, "v", vv)
        v_x2t_all = em.val("v")
        v_cast_done = 0
        for dc in range(16):
            v_cast_done = em.op("vector", lambda v, d=dc: v.tensor_copy(
                x2T_bf[:, d * 1024:(d + 1) * 1024], x2Tf[:, d * 1024:(d + 1) * 1024]),
                waits=([("pe", pe_wo_done), ("dbg", dbg_aoT)] if dc == 0 else ()),
                inc="v")

        pe_router = None
        for tch in range(2):
            ws = bg.write_waits("PA", tch * 512, 512) + [("v", v_x2t_all),
                                                         ("cst", xd_const)]
            for dc in range(16):
                pe_router = em.op("tensor", lambda t, d=dc, t0=tch: t.matmul(
                    PA[0:8, t0 * 512:(t0 + 1) * 512],
                    rw_sb[:, d * 8:(d + 1) * 8],
                    x2Tf[:, d * 1024 + t0 * 512: d * 1024 + t0 * 512 + 512],
                    start=(d == 0), stop=(d == 15)),
                    waits=ws if dc == 0 else (), inc="pe" if dc == 15 else None)
        vv = em.op("vector", lambda v: v.tensor_copy(lg_sb[:, :], PA[0:8, 0:1024]),
                   waits=[("pe", pe_router), ("s", s_ln1_done)], inc="v")
        bg.set_reader("PA", 0, 1024, "v", vv)
        ws = bg.write_waits("PB", 0, 64) + [("v", vv)]
        cc = None
        for tt in range(8):
            cc = em.op("tensor", lambda t, t0=tt: t.transpose(
                PB[:, t0 * 8:(t0 + 1) * 8], lg_sb[0:8, t0 * 128:(t0 + 1) * 128],
                eye_f[0:8, 0:8]), waits=ws if tt == 0 else (),
                inc="pe" if tt == 7 else None)
        vv = em.op("vector", lambda v: v.tensor_copy(lgT_sb[:, :], PB[:, 0:64]),
                   waits=[("pe", cc)], inc="v")
        bg.set_reader("PB", 0, 64, "v", vv)
        s_rt_prev = {}
        for tt in range(8):
            lsl = lgT_sb[:, tt * 8:(tt + 1) * 8]
            psl = probs_sb[:, tt * 8:(tt + 1) * 8]
            pcol = 64 + (tt % 2) * 8
            vm = em.op("vector", lambda v, l=lsl, pc=pcol: v.tensor_reduce(
                stats[:, pc:pc + 1], l, AX.X, ALU.max, negate=True),
                waits=[("s", s_rt_prev.get(tt % 2, 0))], inc="v")
            sc = em.op("scalar", lambda s_, l=lsl, p=psl, pc=pcol: s_.activation(
                p, l, AF.Exp, bias=stats[:, pc:pc + 1]), waits=[("v", vm)], inc="s")
            s_rt_prev[tt % 2] = sc
            vv = em.op("vector", lambda v, p=psl, pc=pcol: v.tensor_reduce(
                stats[:, pc + 1:pc + 2], p, AX.X, ALU.add), waits=[("s", sc)], inc="v")
            sc2 = em.op("scalar", lambda s_, pc=pcol: s_.activation(
                stats[:, pc + 4:pc + 5], stats[:, pc + 1:pc + 2], AF.Copy),
                waits=[("v", vv)], inc="s")
            vv = em.op("vector", lambda v, pc=pcol: v.reciprocal(
                stats[:, pc + 2:pc + 3], stats[:, pc + 4:pc + 5]),
                waits=[("s", sc2)], inc="v")
            sc2 = em.op("scalar", lambda s_, pc=pcol: s_.activation(
                stats[:, pc + 5:pc + 6], stats[:, pc + 2:pc + 3], AF.Copy),
                waits=[("v", vv)], inc="s")
            em.op("vector", lambda v, p=psl, pc=pcol: v.tensor_scalar_mul(
                p, p, stats[:, pc + 5:pc + 6]), waits=[("s", sc2)])
        vv = em.op("vector", lambda v: v.engine_nop(), inc="v")
        sc = em.op("scalar", lambda s_: s_.activation(
            probs2[:, :], probs_sb[:, :], AF.Copy), waits=[("v", vv)], inc="s")
        vv = 0
        for tt in range(8):
            vv = em.op("vector", lambda v, t0=tt: v.tensor_tensor(
                keys_sb[:, t0 * 8:(t0 + 1) * 8], probs2[:, t0 * 8:(t0 + 1) * 8],
                cons[:, 8:16], op=ALU.subtract),
                waits=[("s", sc)] if tt == 0 else (), inc="v")
        sc = em.op("scalar", lambda s_: s_.activation(
            keys2[:, :], keys_sb[:, 0:64], AF.Copy), waits=[("v", vv)], inc="s")
        v_w_done = 0
        for grp in range(2):
            for lt in range(4):
                tt = grp * 4 + lt
                for e in range(NE):
                    em.op("vector", lambda v, t0=tt, l0=lt, e0=e: v.scalar_tensor_tensor(
                        pm_scr[:, l0 * 64 + e0 * 8: l0 * 64 + (e0 + 1) * 8],
                        keys2[:, t0 * 8:(t0 + 1) * 8],
                        keys2[:, t0 * 8 + e0: t0 * 8 + e0 + 1],
                        probs2[:, t0 * 8:(t0 + 1) * 8],
                        op0=ALU.is_gt, op1=ALU.mult),
                        waits=[("s", sc)] if (lt == 0 and e == 0) else ())
            vv = em.op("vector", lambda v: v.engine_nop(), inc="v")
            sc3 = em.op("scalar", lambda s_: s_.activation(
                pm2[:, :], pm_scr[:, :], AF.Copy), waits=[("v", vv)], inc="s")
            for lt in range(4):
                for e in range(NE):
                    em.op("vector", lambda v, l0=lt, e0=e: v.tensor_reduce(
                        S_all[:, l0 * 8 + e0: l0 * 8 + e0 + 1],
                        pm2[:, l0 * 64 + e0 * 8: l0 * 64 + (e0 + 1) * 8],
                        AX.X, ALU.add),
                        waits=[("s", sc3)] if (lt == 0 and e == 0) else ())
            vv = em.op("vector", lambda v: v.engine_nop(), inc="v")
            sc3 = em.op("scalar", lambda s_: s_.activation(
                S_all2[:, :], S_all[:, :], AF.Copy), waits=[("v", vv)], inc="s")
            vv = em.op("vector", lambda v: v.tensor_single_scalar(
                km_all[:, :], S_all2[:, :], 0.9, ALU.is_le),
                waits=[("s", sc3)], inc="v")
            sc3 = em.op("scalar", lambda s_: s_.activation(
                km2[:, :], km_all[:, :], AF.Copy), waits=[("v", vv)], inc="s")
            for lt in range(4):
                tt = grp * 4 + lt
                v_w_done = em.op("vector", lambda v, t0=tt, l0=lt: v.tensor_tensor(
                    w_sb[:, t0 * 8:(t0 + 1) * 8], probs2[:, t0 * 8:(t0 + 1) * 8],
                    km2[:, l0 * 8:(l0 + 1) * 8], op=ALU.mult),
                    waits=[("s", sc3)] if lt == 0 else (), inc="v")
        gu_dma = {}
        gu_last_mm = {}
        dn_dma = {}
        dn_last_mm = {}
        silu_prev = {}
        hid_last_v = 0
        pe_dn_prev = pe_router
        for e in range(NE):
            for ig in range(32):
                gsl = ig % 2
                gu_dma[("g", e, ig)] = em.op("gpsimd", lambda g, e0=e, i0=ig, s0=gsl: g.dma_start(
                    g_ring.rearrange("p (s t c) -> p s t c", s=2, c=128)[:, s0, :, :],
                    gw_v[e0][:, :, i0 * 128:(i0 + 1) * 128]),
                    waits=[("pe", gu_last_mm.get(gsl, 0))], inc="gr%d" % gsl, amt=16)
                gu_dma[("u", e, ig)] = em.op("gpsimd", lambda g, e0=e, i0=ig, s0=gsl: g.dma_start(
                    u_ring.rearrange("p (s t c) -> p s t c", s=2, c=128)[:, s0, :, :],
                    uw_v[e0][:, :, i0 * 128:(i0 + 1) * 128]),
                    waits=[("pe", gu_last_mm.get(gsl, 0))], inc="ur%d" % gsl, amt=16)
                for th in range(2):
                    gpo = th * 512
                    upo = 1024 + th * 512
                    ws = (bg.write_waits("PA", gpo, 512) + bg.write_waits("PA", upo, 512)
                          + [("gr%d" % gsl, gu_dma[("g", e, ig)]),
                             ("ur%d" % gsl, gu_dma[("u", e, ig)]),
                             ("v", v_cast_done)])
                    cu = None
                    for dc in range(16):
                        em.op("tensor", lambda t, d=dc, s0=gsl, t0=th, o=gpo: t.matmul(
                            PA[:, o:o + 512],
                            g_ring[:, s0 * 2048 + d * 128: s0 * 2048 + (d + 1) * 128],
                            x2T_bf[:, d * 1024 + t0 * 512: d * 1024 + t0 * 512 + 512],
                            start=(d == 0), stop=(d == 15)),
                            waits=ws if dc == 0 else ())
                        cu = em.op("tensor", lambda t, d=dc, s0=gsl, t0=th, o=upo: t.matmul(
                            PA[:, o:o + 512],
                            u_ring[:, s0 * 2048 + d * 128: s0 * 2048 + (d + 1) * 128],
                            x2T_bf[:, d * 1024 + t0 * 512: d * 1024 + t0 * 512 + 512],
                            start=(d == 0), stop=(d == 15)),
                            inc="pe" if dc == 15 else None)
                    gu_last_mm[gsl] = cu
                    sc = em.op("scalar", lambda s_, o=gpo, s0=th: s_.activation(
                        silu_scr[:, s0 * 512:(s0 + 1) * 512], PA[:, o:o + 512], AF.Silu),
                        waits=[("pe", cu), ("v", silu_prev.get(th, 0))], inc="s")
                    hid_last_v = em.op("vector", lambda v, i0=ig, t0=th, o=upo, s0=th: v.tensor_tensor(
                        hidT[:, i0 * 1024 + t0 * 512: i0 * 1024 + t0 * 512 + 512],
                        silu_scr[:, s0 * 512:(s0 + 1) * 512], PA[:, o:o + 512],
                        op=ALU.mult),
                        waits=[("s", sc), ("pe", pe_dn_prev)], inc="v")
                    silu_prev[th] = hid_last_v
                    bg.set_reader("PA", gpo, 512, "v", hid_last_v)
                    bg.set_reader("PA", upo, 512, "v", hid_last_v)
            for hg in range(8):
                pd, pdn = (PB, "PB") if hg % 2 == 0 else (PA, "PA")
                for q4 in range(4):
                    slot = (hg * 4 + q4) % 4
                    ws = [("pe", dn_last_mm.get(slot, 0))]
                    if e == 0 and hg == 0:
                        ws += [("pe", pe_a3_done), ("v", v_a3_done)]
                    dn_dma[(e, hg, q4)] = em.op("gpsimd", lambda g, e0=e, h0=hg, f0=q4, s0=slot: g.dma_start(
                        dn_ring.rearrange("p (s t c) -> p s t c", s=4, c=256)[:, s0, :, :],
                        dw_v[e0][:, f0 * 8:(f0 + 1) * 8, h0 * 256:(h0 + 1) * 256]),
                        waits=ws, inc="dr%d" % slot, amt=16)
                ws0 = [("v", hid_last_v)]
                for b4 in range(4):
                    ws0 += bg.write_waits(pdn, b4 * 512, 512)
                cc_last = None
                for ic in range(32):
                    q4 = ic // 8
                    slot = (hg * 4 + q4) % 4
                    it8 = ic % 8
                    for tq in range(NQT):
                        wlist = []
                        if it8 == 0 and tq == 0:
                            wlist.append(("dr%d" % slot, dn_dma[(e, hg, q4)]))
                            if ic == 0:
                                wlist += ws0
                        do_inc = (ic == 31) or (it8 == 7 and tq == NQT - 1)
                        cc = em.op("tensor", lambda t, i0=ic, t0=tq, s0=slot, i1=it8, p=pd: t.matmul(
                            p[:, t0 * 256:(t0 + 1) * 256],
                            hidT[:, i0 * 1024 + t0 * 128: i0 * 1024 + (t0 + 1) * 128],
                            dn_ring[:, s0 * 2048 + i1 * 256: s0 * 2048 + (i1 + 1) * 256],
                            start=(i0 == 0 and t0 % 2 == 0), stop=(i0 == 31)),
                            waits=wlist, inc="pe" if do_inc else None)
                        if cc is not None:
                            cc_last = cc
                    if it8 == 7:
                        dn_last_mm[slot] = cc_last
                for tq in range(NQT):
                    vv = em.op("vector", lambda v, t0=tq, h0=hg, e0=e, p=pd: v.scalar_tensor_tensor(
                        acc[:, t0 * 2048 + h0 * 256: t0 * 2048 + (h0 + 1) * 256],
                        p[:, t0 * 256:(t0 + 1) * 256],
                        w_sb[:, t0 * 8 + e0: t0 * 8 + e0 + 1],
                        acc[:, t0 * 2048 + h0 * 256: t0 * 2048 + (h0 + 1) * 256],
                        op0=ALU.mult, op1=ALU.add),
                        waits=[("pe", cc_last), ("v", v_w_done), ("dbg", dbg_x2)],
                        inc="v")
                    bg.set_reader(pdn, tq * 256, 256, "v", vv)
            pe_dn_prev = em.val("pe")

        v_dn_all = em.val("v")
        dbg_hid = tap("hid7", hidT[:, :], [128, 32 * 1024], BF16, [("v", v_dn_all)])
        dbg_acc3 = tap("acc3", acc[:, :], [128, 8 * 2048], F32, [("v", v_dn_all)])
        for tq in range(NQT):
            at = acc[:, tq * 2048:(tq + 1) * 2048]
            pcol = 80 + (tq % 2) * 8
            vv = em.op("vector", lambda v, a=at, pc=pcol: v.tensor_reduce(
                stats[:, pc:pc + 1], a, AX.X, ALU.add, negate=True),
                waits=[("v", v_dn_all), ("dbg", dbg_acc3)], inc="v")
            sm = em.op("scalar", lambda s_, pc=pcol: s_.activation(
                stats[:, pc + 1:pc + 2], stats[:, pc:pc + 1], AF.Copy,
                scale=1.0 / D), waits=[("v", vv)], inc="s")
            vv = em.op("vector", lambda v, a=at, pc=pcol: v.tensor_scalar_add(
                a, a, stats[:, pc + 1:pc + 2]), waits=[("s", sm)], inc="v")
            s1 = em.op("scalar", lambda s_, a=at,
                       q=(sq_scr if tq % 2 == 0 else sq_scr2): s_.activation(
                q[:, 0:2048], a, AF.Square), waits=[("v", vv)], inc="s")
            vr = em.op("vector", lambda v, pc=pcol,
                       q=(sq_scr if tq % 2 == 0 else sq_scr2): v.tensor_reduce(
                stats[:, pc + 2:pc + 3], q[:, 0:2048], AX.X, ALU.add),
                waits=[("s", s1)], inc="v")
            em.op("scalar", lambda s_, pc=pcol: s_.activation(
                stats[:, pc + 3:pc + 4], stats[:, pc + 2:pc + 3], AF.Ln,
                scale=1.0 / D, bias=cons[:, 1:2]), waits=[("v", vr)])
            sc = em.op("scalar", lambda s_, pc=pcol: s_.activation(
                stats[:, pc + 4:pc + 5], stats[:, pc + 3:pc + 4], AF.Exp, scale=-0.5),
                inc="s")
            vv = em.op("vector", lambda v, a=at, pc=pcol: v.tensor_scalar_mul(
                a, a, stats[:, pc + 4:pc + 5]), waits=[("s", sc)], inc="v")
            em.op("sync", lambda sy, t=tq, a=at: sy.dma_start(out_t[t], a),
                  waits=[("v", vv)], inc="st", amt=16)
        tap("eyebf_end", eye_bf[:, :], [128, 128], BF16, [("st", 16 * NQT)])
        tap("eyef_end", eye_f[:, :], [128, 128], F32, [("st", 16 * NQT)])
        tap("stats_end", stats[:, :], [128, 128], F32, [("st", 16 * NQT)])
        tap("lnbc_end", ln_bc[:, :], [128, 2048], F32, [("st", 16 * NQT)])
        em.op("sync", lambda sy: sy.nop(),
              waits=[("st", 16 * NQT), ("dbg", em.val("dbg"))])

        with nc.Block() as block:
            @block.tensor
            def _(t):
                em.replay(t, "tensor", semh)

            @block.vector
            def _(v):
                em.replay(v, "vector", semh)

            @block.scalar
            def _(s_):
                em.replay(s_, "scalar", semh)

            @block.gpsimd
            def _(g):
                em.replay(g, "gpsimd", semh)

            @block.sync
            def _(sy):
                em.replay(sy, "sync", semh)

    return nc


_PROG_CACHE = {}


def _get_prog():
    if "nc" not in _PROG_CACHE:
        _PROG_CACHE["nc"] = build_program()
    return _PROG_CACHE["nc"]


def _host_inputs(L0_attn_x, L0_attn_ln_w, L0_attn_wq, L0_attn_wk, L0_attn_wv,
                 L0_attn_wo, L0_ffn_router_w, L0_ffn_gate_w, L0_ffn_up_w,
                 L0_ffn_down_w):
    bf = ml_dtypes.bfloat16

    def perm_cols(nheads):
        p = []
        for h in range(nheads):
            base = h * HD
            p += [base + 2 * i for i in range(64)] + [base + 2 * i + 1 for i in range(64)]
        return np.array(p)

    wq_p = np.ascontiguousarray(np.asarray(L0_attn_wq, np.float32)[:, perm_cols(H)])
    wk_p = np.ascontiguousarray(np.asarray(L0_attn_wk, np.float32)[:, perm_cols(KH)])
    wv_c = np.ascontiguousarray(np.asarray(L0_attn_wv, np.float32))
    wo_c = np.ascontiguousarray(np.asarray(L0_attn_wo, np.float32))
    rwT = np.ascontiguousarray(np.asarray(L0_ffn_router_w, np.float32).T)
    gwT = np.ascontiguousarray(np.asarray(L0_ffn_gate_w, np.float32).transpose(0, 2, 1))
    uwT = np.ascontiguousarray(np.asarray(L0_ffn_up_w, np.float32).transpose(0, 2, 1))
    dwT = np.ascontiguousarray(np.asarray(L0_ffn_down_w, np.float32).transpose(0, 2, 1))
    lnbc = np.ascontiguousarray(np.broadcast_to(
        np.asarray(L0_attn_ln_w, np.float32)[None, :], (128, D)))
    eyef = np.eye(128, dtype=np.float32)
    eyebf = np.eye(128, dtype=np.float32).astype(bf)
    cons = np.zeros((128, 16), dtype=np.float32)
    cons[:, 0] = EPS_RMS
    cons[:, 1] = EPS_LN
    cons[:, 8:16] = np.arange(8, dtype=np.float32)[None, :] * 1e-7

    x = np.asarray(L0_attn_x, np.float32)
    inv_freq = (1.0 / 10000.0 ** (np.arange(0, HD, 2, dtype=np.float32) / HD))

    def tables(kv0):
        pos = np.maximum(kv0 + np.arange(TKV), 0).astype(np.float32)
        ang = pos[:, None] * inv_freq[None, :]
        c = np.ascontiguousarray(np.tile(np.cos(ang), (1, 4)).astype(bf))
        s = np.ascontiguousarray(np.tile(np.sin(ang), (1, 4)).astype(bf))
        return c, s

    def mk_masks(kv0):
        mk = np.full((8, 128, 640), NEGINF, dtype=np.float32)
        ql = np.arange(128)[:, None]
        t = np.arange(640)[None, :]
        band = (t >= ql) & (t <= ql + WIN)
        for j in range(2):
            for qc in range(4):
                kabs = kv0 + j * 512 + qc * 128 + t
                valid = band & (kabs >= 0)
                mk[j * 4 + qc][valid] = 0.0
        return np.ascontiguousarray(mk.astype(bf))

    tabs = {kv0: tables(kv0) for kv0 in (-512, 512)}
    mks = {kv0: mk_masks(kv0) for kv0 in (-512, 512)}

    in_maps = []
    for c in range(8):
        b, m = c // 2, c % 2
        kv0 = -512 if m == 0 else 512
        if m == 0:
            xkv_c = np.concatenate([np.zeros((512, D), np.float32), x[b, 0:1024]], 0)
        else:
            xkv_c = x[b, 512:2048]
        cos_t, sin_t = tabs[kv0]
        in_maps.append({
            "xkv": np.ascontiguousarray(xkv_c), "lnbc": lnbc,
            "wq": wq_p, "wk": wk_p, "wv": wv_c, "wo": wo_c,
            "rwT": rwT, "gwT": gwT, "uwT": uwT, "dwT": dwT,
            "cos4": cos_t, "sin4": sin_t, "masks": mks[kv0],
            "eyef": eyef, "eyebf": eyebf, "cons": cons,
        })
    return in_maps


def kernel(**inputs):
    nc = _get_prog()
    in_maps = _host_inputs(**inputs)
    res = run_bass_kernel_spmd(nc, in_maps, core_ids=list(range(8)))
    out = np.empty((4, 2048, D), dtype=np.float32)
    for c in range(8):
        b, m = c // 2, c % 2
        out[b, m * 1024:(m + 1) * 1024] = res.results[c]["out"]
    return out
